# version 03 (final)
# baseline (speedup 1.0000x reference)
import numpy as np
import ml_dtypes
import concourse.bass as bass
import concourse.mybir as mybir
from concourse.bass_utils import run_bass_kernel_spmd
from contextlib import ExitStack

F32 = mybir.dt.float32
BF16 = mybir.dt.bfloat16
AF = mybir.ActivationFunctionType
ALU = mybir.AluOpType
AX = mybir.AxisListType

NC = 8
B = 4
D = 1024
LC = 1024
TOK = B * LC
EPS = 1e-5
BF = ml_dtypes.bfloat16

ENGINES = ("sync", "vector", "scalar", "tensor", "gpsimd")


class Seq:

    def __init__(self):
        self.steps = []
        self._force_next = False
        self.counts = {e: 0 for e in ENGINES}

    def em(self, engine, fn, dma=False, after=None):
        inc = 16 if dma else 1
        self.steps.append([engine, fn, inc, after, self._force_next])
        self._force_next = False
        self.counts[engine] += inc
        return (engine, self.counts[engine])

    def brk(self):
        self._force_next = True

    def build(self, nc, blk, stack):
        sems = {e: stack.enter_context(nc.semaphore(f"sq_{e}")) for e in ENGINES}
        for e, c in self.counts.items():
            assert c < 60000, f"semaphore overflow risk on {e}: {c}"
        cur = {e: 0 for e in ENGINES}
        sched = []
        wm = {e: {p: 0 for p in ENGINES} for e in ENGINES}
        for engine, fn, inc, after, force in self.steps:
            waits = []
            if inc == 16 and cur[engine] > wm[engine][engine]:
                waits.append((engine, cur[engine]))
                wm[engine][engine] = cur[engine]
            if after is None:
                for p in ENGINES:
                    if p == engine and not force:
                        continue
                    tgt = cur[p]
                    if tgt > wm[engine][p]:
                        waits.append((p, tgt))
                        wm[engine][p] = tgt
            else:
                if force:
                    after = list(after) + [(engine, cur[engine])]
                for (p, tgt) in after:
                    if p == engine:
                        tgt = min(tgt, cur[engine])
                    if tgt > wm[engine][p]:
                        waits.append((p, tgt))
                        wm[engine][p] = tgt
            sched.append((engine, fn, inc, waits))
            cur[engine] += inc

        def body_for(engname):
            def body(eng):
                for engine, fn, inc, waits in sched:
                    if engine != engname:
                        continue
                    for (p, tgt) in waits:
                        eng.wait_ge(sems[p], tgt)
                    inst = fn(eng)
                    inst.then_inc(sems[engine], inc)
                for p in ENGINES:
                    if self.counts[p] > 0:
                        eng.wait_ge(sems[p], self.counts[p])
            return body
        blk.sync(body_for("sync"))
        blk.vector(body_for("vector"))
        blk.scalar(body_for("scalar"))
        blk.tensor(body_for("tensor"))
        blk.gpsimd(body_for("gpsimd"))


def fmj(t, b=None, c0=None, w=None):
    ap = t.rearrange("b (c p) l -> b p c l", p=128)
    return ap[b, :, :, c0:c0 + w]



def build_l1ffn_pre():
    nc = bass.Bass("TRN2", target_bir_lowering=False)
    xsh = nc.dram_tensor("xsh", [B, D, LC], F32, kind="ExternalInput").ap()
    cenT = nc.dram_tensor("cenT", [D, 8], F32, kind="ExternalInput").ap()
    xhat_o = nc.dram_tensor("xhat_o", [D, TOK], BF16, kind="ExternalOutput").ap()
    feats_o = nc.dram_tensor("feats_o", [D, TOK], F32, kind="ExternalOutput").ap()
    w_o = nc.dram_tensor("w_o", [TOK, 8], F32, kind="ExternalOutput").ap()

    W = 512
    NT = TOK // W

    with ExitStack() as stack:
        E = stack.enter_context
        xt = E(nc.sbuf_tensor("xt", [128, 8 * W], F32))
        xb = E(nc.sbuf_tensor("xb", [128, 8 * W], BF16))
        tmpf = E(nc.sbuf_tensor("tmpf", [128, 8 * W], F32))
        sq = E(nc.sbuf_tensor("sq", [128, 8 * W], BF16))
        feats = E(nc.sbuf_tensor("feats", [128, 8 * W], F32))
        fb = E(nc.sbuf_tensor("fb", [128, 8 * W], BF16))
        fsq = E(nc.sbuf_tensor("fsq", [128, 8 * W], BF16))
        xhat = E(nc.sbuf_tensor("xhat", [128, 8 * W], BF16))
        ones1 = E(nc.sbuf_tensor("ones1", [128, 1], BF16))
        onesb = E(nc.sbuf_tensor("onesb", [1, 128], F32))
        cen_s = E(nc.sbuf_tensor("cen_s", [128, 8 * 8], F32))
        cen_b = E(nc.sbuf_tensor("cen_b", [128, 8 * 8], BF16))
        rows = E(nc.sbuf_tensor("rows", [1, 6 * W], F32))
        tmor = E(nc.sbuf_tensor("tmor", [128, 8], F32))
        mask = E(nc.sbuf_tensor("mask", [128, 8], F32))
        sig = E(nc.sbuf_tensor("sig", [128, 8], F32))
        wv = E(nc.sbuf_tensor("wv", [128, 8], F32))
        ps_sum = E(nc.psum_tensor("ps_sum", [1, W], F32))
        ps_sq = E(nc.psum_tensor("ps_sq", [1, W], F32))
        ps_b1 = E(nc.psum_tensor("ps_b1", [128, W], F32))
        ps_b2 = E(nc.psum_tensor("ps_b2", [128, W], F32))
        ps_aff = E(nc.psum_tensor("ps_aff", [128, 32], F32))

        s = Seq()
        s.em("vector", lambda v: v.memset(ones1[:, :], 1.0))
        s.em("vector", lambda v: v.memset(onesb[:, :], 1.0))
        s.em("sync", lambda e: e.dma_start(out=cen_s[:, :].rearrange("p (c e) -> p c e", c=8),
             in_=cenT.rearrange("(c p) e -> p c e", p=128)), dma=True)
        s.em("vector", lambda v: v.tensor_copy(cen_b[:, :], cen_s[:, :]))

        def ln_pass(src_f32, src_bf, sq_bf, dst_f32, dst_bf, affine=False):
            s.em("vector", lambda v: v.tensor_copy(src_bf[:, :], src_f32[:, :]))
            s.em("scalar", lambda sc: sc.activation(sq_bf[:, :], src_f32[:, :], AF.Square))
            for c in range(8):
                cc = c
                s.em("tensor", lambda t, cc=cc: t.matmul(
                    ps_sum[:, :], ones1[:, :], src_bf[:, cc * W:(cc + 1) * W],
                    start=(cc == 0), stop=(cc == 7)))
            for c in range(8):
                cc = c
                s.em("tensor", lambda t, cc=cc: t.matmul(
                    ps_sq[:, :], ones1[:, :], sq_bf[:, cc * W:(cc + 1) * W],
                    start=(cc == 0), stop=(cc == 7)))
            mu = rows[:, 0 * W:1 * W]
            msq = rows[:, 1 * W:2 * W]
            var = rows[:, 2 * W:3 * W]
            rstd = rows[:, 3 * W:4 * W]
            r2 = rows[:, 4 * W:5 * W]
            tmp = rows[:, 5 * W:6 * W]
            s.em("vector", lambda v: v.tensor_scalar_mul(mu, ps_sum[:, :], 1.0 / D))
            s.em("vector", lambda v: v.tensor_scalar_mul(msq, ps_sq[:, :], 1.0 / D))
            s.em("vector", lambda v: v.tensor_mul(tmp, mu, mu))
            s.em("vector", lambda v: v.tensor_sub(var, msq, tmp))
            s.em("vector", lambda v: v.tensor_scalar_add(var, var, EPS))
            s.em("scalar", lambda sc: sc.activation(var, var, AF.Sqrt))
            s.brk()
            s.em("vector", lambda v: v.reciprocal(rstd, var))
            s.brk()
            s.em("vector", lambda v: v.tensor_mul(tmp, mu, rstd))
            s.em("vector", lambda v: v.tensor_scalar_mul(r2, tmp, -1.0))
            s.em("tensor", lambda t: t.matmul(ps_b1[:, :], onesb[:, :], rstd, start=True, stop=True))
            s.em("tensor", lambda t: t.matmul(ps_b2[:, :], onesb[:, :], r2, start=True, stop=True))
            for c in range(8):
                cc = c
                sl = slice(cc * W, (cc + 1) * W)
                s.em("vector", lambda v, sl=sl: v.tensor_mul(tmpf[:, sl], src_f32[:, sl], ps_b1[:, :]))
                if dst_f32 is not None:
                    s.em("vector", lambda v, sl=sl: v.tensor_add(dst_f32[:, sl], tmpf[:, sl], ps_b2[:, :]))
                    s.em("vector", lambda v, sl=sl: v.tensor_copy(dst_bf[:, sl], dst_f32[:, sl]))
                else:
                    s.em("vector", lambda v, sl=sl: v.tensor_add(dst_bf[:, sl], tmpf[:, sl], ps_b2[:, :]))

        for ti in range(NT):
            b = ti // 2
            c0 = (ti % 2) * W
            t0 = ti * W
            s.em("sync", lambda e, b=b, c0=c0: e.dma_start(
                out=xt[:, :].rearrange("p (c l) -> p c l", c=8), in_=fmj(xsh, b, c0, W)), dma=True)
            ln_pass(xt, xb, sq, feats, fb)
            ln_pass(feats, fb, fsq, None, xhat)
            for st_ in range(4):
                for c in range(8):
                    st2, cc = st_, c
                    s.em("tensor", lambda t, st2=st2, cc=cc: t.matmul(
                        ps_aff[:, st2 * 8:(st2 + 1) * 8],
                        feats[:, cc * W + st2 * 128: cc * W + st2 * 128 + 128],
                        cen_s[:, cc * 8:(cc + 1) * 8],
                        start=(cc == 0), stop=(cc == 7)))
            for st_ in range(4):
                st2 = st_
                sl = slice(st2 * 8, (st2 + 1) * 8)
                s.brk()
                s.em("vector", lambda v, sl=sl: v.reduce_max(tmor[:, 0:1], ps_aff[:, sl], axis=AX.X))
                s.brk()
                s.em("vector", lambda v, sl=sl: v.tensor_tensor(
                    mask[:, 0:8], ps_aff[:, sl], tmor[:, 0:1].broadcast_to((128, 8)), ALU.is_equal))
                s.em("scalar", lambda sc, sl=sl: sc.activation(sig[:, 0:8], ps_aff[:, sl], AF.Sigmoid))
                s.em("vector", lambda v: v.tensor_mul(wv[:, 0:8], mask[:, 0:8], sig[:, 0:8]))
                s.em("gpsimd", lambda g, st2=st2, t0=t0: g.dma_start(
                    out=w_o[t0 + st2 * 128: t0 + st2 * 128 + 128, :], in_=wv[:, 0:8]), dma=True)
            s.em("gpsimd", lambda g, t0=t0: g.dma_start(
                out=xhat_o.rearrange("(c p) l -> p c l", p=128)[:, :, t0:t0 + W],
                in_=xhat[:, :].rearrange("p (c l) -> p c l", c=8)), dma=True)
            s.em("gpsimd", lambda g, t0=t0: g.dma_start(
                out=feats_o.rearrange("(c p) l -> p c l", p=128)[:, :, t0:t0 + W],
                in_=feats[:, :].rearrange("p (c l) -> p c l", c=8)), dma=True)

        blk = E(nc.Block())
        s.build(nc, blk, stack)
    return nc



def build_l1ffn_expert():
    nc = bass.Bass("TRN2", target_bir_lowering=False)
    NTOK = NC * TOK
    F = 4096
    xhat_all = nc.dram_tensor("xhat_all", [D, NTOK], BF16, kind="ExternalInput").ap()
    wcol = nc.dram_tensor("wcol", [NTOK], F32, kind="ExternalInput").ap()
    w1 = nc.dram_tensor("w1", [D, F], F32, kind="ExternalInput").ap()
    w2 = nc.dram_tensor("w2", [F, D], F32, kind="ExternalInput").ap()
    b1 = nc.dram_tensor("b1", [F], F32, kind="ExternalInput").ap()
    b2 = nc.dram_tensor("b2", [D], F32, kind="ExternalInput").ap()
    gam = nc.dram_tensor("gam", [D], F32, kind="ExternalInput").ap()
    bet = nc.dram_tensor("bet", [D], F32, kind="ExternalInput").ap()
    contrib = nc.dram_tensor("contrib", [D, NTOK], BF16, kind="ExternalOutput").ap()

    W = 512
    NT = NTOK // W

    with ExitStack() as stack:
        E = stack.enter_context
        w1f = E(nc.sbuf_tensor("w1f", [128, F], F32))
        w1b = E(nc.sbuf_tensor("w1b", [128, 8 * F], BF16))
        w2b = E(nc.sbuf_tensor("w2b", [128, 32 * D], BF16))
        cols = E(nc.sbuf_tensor("cols", [128, 8 * 4], F32))
        b1c = E(nc.sbuf_tensor("b1c", [128, 32], F32))
        xt = E(nc.sbuf_tensor("xt", [128, 8 * W], BF16))
        xg = E(nc.sbuf_tensor("xg", [128, 8 * W], BF16))
        h1r = E(nc.sbuf_tensor("h1r", [128, 32 * W], BF16))
        wrow = E(nc.sbuf_tensor("wrow", [1, W], F32))
        wbc = E(nc.sbuf_tensor("wbc", [128, W], F32))
        ct = E(nc.sbuf_tensor("ct", [128, 8 * W], BF16))
        onesb = E(nc.sbuf_tensor("onesb", [1, 128], F32))
        ps_h = E(nc.psum_tensor("ps_h", [128, 4 * W], F32))
        ps_y = E(nc.psum_tensor("ps_y", [128, 4 * W], F32))

        s = Seq()
        s.em("vector", lambda v: v.memset(onesb[:, :], 1.0))
        for c in range(8):
            cc = c
            s.em("sync", lambda e, cc=cc: e.dma_start(out=w1f[:, :], in_=w1[cc * 128:(cc + 1) * 128, :]), dma=True)
            s.brk()
            s.em("vector", lambda v, cc=cc: v.tensor_copy(w1b[:, cc * F:(cc + 1) * F], w1f[:, :]))
            s.brk()
        for c in range(32):
            cc = c
            s.em("sync", lambda e, cc=cc: e.dma_start(out=w1f[:, 0:D], in_=w2[cc * 128:(cc + 1) * 128, :]), dma=True)
            s.brk()
            s.em("vector", lambda v, cc=cc: v.tensor_copy(w2b[:, cc * D:(cc + 1) * D], w1f[:, 0:D]))
            s.brk()
        s.em("sync", lambda e: e.dma_start(out=cols[:, 0:8], in_=gam.rearrange("(c p) -> p c", p=128), allow_slow_non_contiguous=True), dma=True)
        s.em("sync", lambda e: e.dma_start(out=cols[:, 8:16], in_=bet.rearrange("(c p) -> p c", p=128), allow_slow_non_contiguous=True), dma=True)
        s.em("sync", lambda e: e.dma_start(out=cols[:, 16:24], in_=b2.rearrange("(c p) -> p c", p=128), allow_slow_non_contiguous=True), dma=True)
        s.em("sync", lambda e: e.dma_start(out=b1c[:, :], in_=b1.rearrange("(c p) -> p c", p=128), allow_slow_non_contiguous=True), dma=True)

        h_p1mm_last = None
        h_evac_last = None
        h_ct_last = None
        h_store_last = None
        h_wmm_last = None
        for ti in range(NT):
            t0 = ti * W
            aft_load = [h_ct_last] if h_ct_last else []
            h_lx = s.em("sync", lambda e, t0=t0: e.dma_start(
                out=xt[:, :].rearrange("p (c l) -> p c l", c=8),
                in_=xhat_all.rearrange("(c p) l -> p c l", p=128)[:, :, t0:t0 + W]),
                dma=True, after=aft_load)
            h_lw = s.em("sync", lambda e, t0=t0: e.dma_start(
                out=wrow[:, :], in_=wcol[t0:t0 + W].rearrange("(o l) -> o l", o=1)),
                dma=True, after=[h_wmm_last] if h_wmm_last else [])
            aft_xg = [h_lx] + ([h_p1mm_last] if h_p1mm_last else [])
            h_xg = None
            for c in range(8):
                cc = c
                sl = slice(cc * W, (cc + 1) * W)
                h_xg = s.em("vector", lambda v, sl=sl, cc=cc: v.tensor_scalar(
                    xg[:, sl], xt[:, sl], cols[:, cc:cc + 1], cols[:, 8 + cc:9 + cc],
                    ALU.mult, ALU.add), after=aft_xg)
            aft_wbc = [h_lw] + ([h_evac_last] if h_evac_last else [])
            h_wmm = s.em("tensor", lambda t: t.matmul(
                ps_h[:, 0:W], onesb[:, :], wrow[:, :], start=True, stop=True), after=aft_wbc)
            h_wmm_last = h_wmm
            h_wbc = s.em("vector", lambda v: v.tensor_copy(wbc[:, :], ps_h[:, 0:W]), after=[h_wmm])
            evac_h = {}
            mm_h = {}
            for mg in range(8):
                for mi in range(4):
                    mm = mg * 4 + mi
                    psl = slice(mi * W, (mi + 1) * W)
                    aft = [h_xg]
                    if mg == 0:
                        aft.append(h_wbc)
                        if h_evac_last:
                            aft.append(h_evac_last)
                    else:
                        aft.append(evac_h[(mg - 1, mi)])
                    hh = None
                    for c in range(8):
                        cc = c
                        hh = s.em("tensor", lambda t, mm=mm, cc=cc, psl=psl: t.matmul(
                            ps_h[:, psl],
                            w1b[:, cc * F + mm * 128: cc * F + mm * 128 + 128],
                            xg[:, cc * W:(cc + 1) * W],
                            start=(cc == 0), stop=(cc == 7)), after=aft if cc == 0 else [])
                    mm_h[(mg, mi)] = hh
                for mi in range(4):
                    mm = mg * 4 + mi
                    psl = slice(mi * W, (mi + 1) * W)
                    evac_h[(mg, mi)] = s.em("scalar", lambda sc, mm=mm, psl=psl: sc.activation(
                        h1r[:, mm * W:(mm + 1) * W], ps_h[:, psl], AF.Relu,
                        bias=b1c[:, mm:mm + 1]), after=[mm_h[(mg, mi)]])
            h_p1mm_last = mm_h[(7, 3)]
            ct_h = {}
            for dg in range(2):
                for di in range(4):
                    d2 = dg * 4 + di
                    psl = slice(di * W, (di + 1) * W)
                    aft = [evac_h[(7, 3)]]
                    if dg == 1:
                        aft.append(ct_h[(0, di)])
                    elif h_ct_last is not None:
                        aft.append(h_ct_last)
                    hh = None
                    for m in range(32):
                        mm = m
                        hh = s.em("tensor", lambda t, d2=d2, mm=mm, psl=psl: t.matmul(
                            ps_y[:, psl],
                            w2b[:, mm * D + d2 * 128: mm * D + d2 * 128 + 128],
                            h1r[:, mm * W:(mm + 1) * W],
                            start=(mm == 0), stop=(mm == 31)), after=aft if mm == 0 else [])
                    mm_h[("y", dg, di)] = hh
                for di in range(4):
                    d2 = dg * 4 + di
                    psl = slice(di * W, (di + 1) * W)
                    aft = [mm_h[("y", dg, di)], h_wbc]
                    if h_store_last is not None:
                        aft.append(h_store_last)
                    ct_h[(dg, di)] = s.em("vector", lambda v, d2=d2, psl=psl: v.scalar_tensor_tensor(
                        ct[:, d2 * W:(d2 + 1) * W], ps_y[:, psl], cols[:, 16 + d2:17 + d2], wbc[:, :],
                        ALU.add, ALU.mult), after=aft)
            h_ct_last = ct_h[(1, 3)]
            h_store_last = s.em("gpsimd", lambda g, t0=t0: g.dma_start(
                out=contrib.rearrange("(c p) l -> p c l", p=128)[:, :, t0:t0 + W],
                in_=ct[:, :].rearrange("p (c l) -> p c l", c=8)), dma=True, after=[h_ct_last])

        blk = E(nc.Block())
        s.build(nc, blk, stack)
    return nc



def build_l1ffn_post():
    nc = bass.Bass("TRN2", target_bir_lowering=False)
    xsh = nc.dram_tensor("xsh", [B, D, LC], F32, kind="ExternalInput").ap()
    feats = nc.dram_tensor("feats", [D, TOK], F32, kind="ExternalInput").ap()
    contribs = nc.dram_tensor("contribs", [8, D, TOK], BF16, kind="ExternalInput").ap()
    xout = nc.dram_tensor("xout", [B, D, LC], F32, kind="ExternalOutput").ap()

    W = 512
    NT = TOK // W

    with ExitStack() as stack:
        E = stack.enter_context
        xt = E(nc.sbuf_tensor("xt", [128, 8 * W], F32))
        ft = E(nc.sbuf_tensor("ft", [128, 8 * W], F32))
        cts = [E(nc.sbuf_tensor(f"ct{e}", [128, 8 * W], BF16)) for e in range(8)]
        acc = E(nc.sbuf_tensor("acc", [128, 8 * W], F32))
        out = E(nc.sbuf_tensor("out", [128, 8 * W], F32))

        s = Seq()
        for ti in range(NT):
            b = ti // 2
            c0 = (ti % 2) * W
            t0 = ti * W
            s.em("sync", lambda e, b=b, c0=c0: e.dma_start(
                out=xt[:, :].rearrange("p (c l) -> p c l", c=8), in_=fmj(xsh, b, c0, W)), dma=True)
            s.em("sync", lambda e, t0=t0: e.dma_start(
                out=ft[:, :].rearrange("p (c l) -> p c l", c=8),
                in_=feats.rearrange("(c p) l -> p c l", p=128)[:, :, t0:t0 + W]), dma=True)
            for e_ in range(8):
                ee = e_
                s.em("sync", lambda en, ee=ee, t0=t0: en.dma_start(
                    out=cts[ee][:, :].rearrange("p (c l) -> p c l", c=8),
                    in_=contribs.rearrange("e (c p) l -> e p c l", p=128)[ee, :, :, t0:t0 + W]), dma=True)
            s.em("vector", lambda v: v.tensor_add(acc[:, :], cts[0][:, :], cts[1][:, :]))
            for e_ in range(2, 8):
                ee = e_
                s.em("vector", lambda v, ee=ee: v.tensor_add(acc[:, :], acc[:, :], cts[ee][:, :]))
            s.em("vector", lambda v: v.tensor_add(acc[:, :], acc[:, :], ft[:, :]))
            s.em("vector", lambda v: v.scalar_tensor_tensor(
                out[:, :], acc[:, :], 0.5, xt[:, :], ALU.mult, ALU.add))
            s.em("gpsimd", lambda g, b=b, c0=c0: g.dma_start(
                out=fmj(xout, b, c0, W), in_=out[:, :].rearrange("p (c l) -> p c l", c=8)), dma=True)

        blk = E(nc.Block())
        s.build(nc, blk, stack)
    return nc


_progs = {}


def get_prog(name, builder):
    if name not in _progs:
        _progs[name] = builder()
    return _progs[name]


def run_l1ffn(x_fm, centroids, w1, b1, w2, b2, gamma, beta):
    cenT = np.ascontiguousarray(centroids.T)
    p7 = get_prog("l1ffn_pre", build_l1ffn_pre)
    r = run_bass_kernel_spmd(
        p7, [dict(xsh=x_fm[c], cenT=cenT) for c in range(NC)], core_ids=list(range(NC)))
    xhat = [r.results[c]["xhat_o"] for c in range(NC)]
    feats = [r.results[c]["feats_o"] for c in range(NC)]
    wmat = [r.results[c]["w_o"] for c in range(NC)]
    xhat_all = np.concatenate(xhat, axis=1)
    wall = np.concatenate(wmat, axis=0)
    p8 = get_prog("l1ffn_expert", build_l1ffn_expert)
    in8 = [dict(xhat_all=xhat_all, wcol=np.ascontiguousarray(wall[:, e]),
                w1=w1[e], w2=w2[e], b1=b1[e], b2=b2[e], gam=gamma[e], bet=beta[e])
           for e in range(8)]
    r8 = run_bass_kernel_spmd(p8, in8, core_ids=list(range(NC)))
    contrib = [r8.results[e]["contrib"] for e in range(8)]
    p9 = get_prog("l1ffn_post", build_l1ffn_post)
    in9 = []
    for c in range(NC):
        sl = slice(c * TOK, (c + 1) * TOK)
        in9.append(dict(xsh=x_fm[c], feats=feats[c],
                        contribs=np.ascontiguousarray(np.stack([ct[:, sl] for ct in contrib], 0))))
    r9 = run_bass_kernel_spmd(p9, in9, core_ids=list(range(NC)))
    return [r9.results[c]["xout"] for c in range(NC)]


def kernel(**inputs):
    x = np.ascontiguousarray(inputs['L0_attn_x'].astype(np.float32, copy=False))
    x_fm = [np.ascontiguousarray(x[:, c * LC:(c + 1) * LC, :].transpose(0, 2, 1)) for c in range(NC)]
    phi2d = np.ascontiguousarray(inputs['L0_ffn_phi'].reshape(D, 2048))
    x_fm = run_l0ffn(x_fm, phi2d, inputs['L0_ffn_W'], inputs['L0_ffn_b'])
    x_fm = run_mamba(x_fm, inputs['L0_attn_W_in'], inputs['L0_attn_W_conv'], inputs['L0_attn_b_conv'],
                     inputs['L0_attn_W_xproj'], inputs['L0_attn_W_dt'], inputs['L0_attn_b_dt'],
                     inputs['L0_attn_A_log'], inputs['L0_attn_ln_gamma'], inputs['L0_attn_ln_beta'],
                     inputs['L0_attn_W_out'], inputs['L0_attn_b_out'])
    x_fm = run_l0ffn(x_fm, phi2d, inputs['L0_ffn_W'], inputs['L0_ffn_b'])
    x_fm = run_l1ffn(x_fm, inputs['L1_ffn_centroids'], inputs['L1_ffn_w1'], inputs['L1_ffn_b1'],
                     inputs['L1_ffn_w2'], inputs['L1_ffn_b2'], inputs['L1_ffn_gamma'], inputs['L1_ffn_beta'])
    x_fm = run_l1attn(x_fm, inputs)
    x_fm = run_l1ffn(x_fm, inputs['L1_ffn_centroids'], inputs['L1_ffn_w1'], inputs['L1_ffn_b1'],
                     inputs['L1_ffn_w2'], inputs['L1_ffn_b2'], inputs['L1_ffn_gamma'], inputs['L1_ffn_beta'])
    out = np.concatenate([o.transpose(0, 2, 1) for o in x_fm], axis=1)
    return np.ascontiguousarray(out.astype(np.float32, copy=False))



class LNKit:
    def __init__(self, nc, E, W, tag="", bank=None, sqb=None, srcb=None, tmpf=None):
        self.W = W
        self.ones1 = E(nc.sbuf_tensor(f"ln_ones1{tag}", [128, 1], BF16))
        self.onesb = E(nc.sbuf_tensor(f"ln_onesb{tag}", [1, 128], F32))
        self.rows = E(nc.sbuf_tensor(f"ln_rows{tag}", [1, 6 * W], F32))
        self.sqb = sqb if sqb is not None else E(nc.sbuf_tensor(f"ln_sqb{tag}", [128, 8 * W], BF16))
        self.srcb = srcb if srcb is not None else E(nc.sbuf_tensor(f"ln_srcb{tag}", [128, 8 * W], BF16))
        self.tmpf = tmpf if tmpf is not None else E(nc.sbuf_tensor(f"ln_tmpf{tag}", [128, 8 * W], F32))
        if bank is None:
            bank = E(nc.psum_tensor(f"ln_bank{tag}", [128, 2 * W], F32))[:, :]
        self.ps_sum = bank[0:1, 0:W]
        self.ps_sq = bank[0:1, W:2 * W]
        self.ps_b1 = bank[:, 0:W]
        self.ps_b2 = bank[:, W:2 * W]

    def emit_init(self, s):
        s.em("vector", lambda v: v.memset(self.ones1[:, :], 1.0))
        s.em("vector", lambda v: v.memset(self.onesb[:, :], 1.0))

    def emit(self, s, src_f32, dst_bf, dst_f32=None, gam_col=None, bet_col=None):
        W = self.W
        rows = self.rows
        s.em("vector", lambda v: v.tensor_copy(self.srcb[:, :], src_f32[:, :]))
        s.em("scalar", lambda sc: sc.activation(self.sqb[:, :], src_f32[:, :], AF.Square))
        for c in range(8):
            cc = c
            s.em("tensor", lambda t, cc=cc: t.matmul(
                self.ps_sum, self.ones1[:, :], self.srcb[:, cc * W:(cc + 1) * W],
                start=(cc == 0), stop=(cc == 7)))
        for c in range(8):
            cc = c
            s.em("tensor", lambda t, cc=cc: t.matmul(
                self.ps_sq, self.ones1[:, :], self.sqb[:, cc * W:(cc + 1) * W],
                start=(cc == 0), stop=(cc == 7)))
        mu = rows[:, 0 * W:1 * W]
        msq = rows[:, 1 * W:2 * W]
        var = rows[:, 2 * W:3 * W]
        rstd = rows[:, 3 * W:4 * W]
        r2 = rows[:, 4 * W:5 * W]
        tmp = rows[:, 5 * W:6 * W]
        s.em("vector", lambda v: v.tensor_scalar_mul(mu, self.ps_sum, 1.0 / D))
        s.em("vector", lambda v: v.tensor_scalar_mul(msq, self.ps_sq, 1.0 / D))
        s.em("vector", lambda v: v.tensor_mul(tmp, mu, mu))
        s.em("vector", lambda v: v.tensor_sub(var, msq, tmp))
        s.em("vector", lambda v: v.tensor_scalar_add(var, var, EPS))
        s.em("scalar", lambda sc: sc.activation(var, var, AF.Sqrt))
        s.brk()
        s.em("vector", lambda v: v.reciprocal(rstd, var))
        s.brk()
        s.em("vector", lambda v: v.tensor_mul(tmp, mu, rstd))
        s.em("vector", lambda v: v.tensor_scalar_mul(r2, tmp, -1.0))
        s.em("tensor", lambda t: t.matmul(self.ps_b1, self.onesb[:, :], rstd, start=True, stop=True))
        s.em("tensor", lambda t: t.matmul(self.ps_b2, self.onesb[:, :], r2, start=True, stop=True))
        for c in range(8):
            cc = c
            sl = slice(cc * W, (cc + 1) * W)
            s.em("vector", lambda v, sl=sl: v.tensor_mul(self.tmpf[:, sl], src_f32[:, sl], self.ps_b1))
            if gam_col is not None:
                s.em("vector", lambda v, sl=sl: v.tensor_add(self.tmpf[:, sl], self.tmpf[:, sl], self.ps_b2))
                s.em("vector", lambda v, sl=sl, cc=cc: v.tensor_scalar(
                    (dst_f32 if dst_f32 is not None else dst_bf)[:, sl], self.tmpf[:, sl],
                    gam_col(cc), bet_col(cc), ALU.mult, ALU.add))
                if dst_f32 is not None:
                    s.em("vector", lambda v, sl=sl: v.tensor_copy(dst_bf[:, sl], dst_f32[:, sl]))
            elif dst_f32 is not None:
                s.em("vector", lambda v, sl=sl: v.tensor_add(dst_f32[:, sl], self.tmpf[:, sl], self.ps_b2))
                s.em("vector", lambda v, sl=sl: v.tensor_copy(dst_bf[:, sl], dst_f32[:, sl]))
            else:
                s.em("vector", lambda v, sl=sl: v.tensor_add(dst_bf[:, sl], self.tmpf[:, sl], self.ps_b2))



def build_ffnA():
    nc = bass.Bass("TRN2", target_bir_lowering=False)
    NP = 2048
    xsh = nc.dram_tensor("xsh", [B, D, LC], F32, kind="ExternalInput").ap()
    phi = nc.dram_tensor("phi", [D, NP], F32, kind="ExternalInput").ap()
    slots_p = nc.dram_tensor("slots_p", [B, D, NP], F32, kind="ExternalOutput").ap()
    logits_o = nc.dram_tensor("logits_o", [TOK, NP], F32, kind="ExternalOutput").ap()
    xhat_h = nc.dram_tensor("xhat_h", [D, TOK], BF16, kind="Internal").ap()
    phib_h = nc.dram_tensor("phib_h", [D, NP], BF16, kind="Internal").ap()

    WL = 256
    with ExitStack() as stack:
        E = stack.enter_context
        xt = E(nc.sbuf_tensor("xt", [128, 8 * WL], F32))
        xhat = E(nc.sbuf_tensor("xhat", [128, 8 * WL], BF16))
        phif = E(nc.sbuf_tensor("phif", [128, NP], F32))
        phic = E(nc.sbuf_tensor("phic", [128, NP], BF16))
        phin = E(nc.sbuf_tensor("phin", [128, 8 * 512], BF16))
        xT = E(nc.sbuf_tensor("xT", [128, 32 * 1024], BF16))
        xf = E(nc.sbuf_tensor("xf", [128, 8 * 128], BF16))
        lgs = E(nc.sbuf_tensor("lgs", [128, 4 * 512], F32))
        lrow = E(nc.sbuf_tensor("lrow", [128, 3 * 512], F32))
        expv = E(nc.sbuf_tensor("expv", [128, 4 * 512], F32))
        disp = E(nc.sbuf_tensor("disp", [128, 32 * 512], BF16))
        sevac = E(nc.sbuf_tensor("sevac", [128, 512], F32))
        ps_all = E(nc.psum_tensor("ps_all", [128, 8 * 512], F32))
        ps_lg = ps_all[:, 0:1024]
        ps_sl = ps_all[:, :]
        ln = LNKit(nc, E, WL, bank=ps_all[:, 2048:2048 + 2 * WL])

        s = Seq()
        ln.emit_init(s)
        for c in range(8):
            cc = c
            s.em("sync", lambda e, cc=cc: e.dma_start(out=phif[:, :], in_=phi[cc * 128:(cc + 1) * 128, :]), dma=True)
            s.brk()
            s.em("vector", lambda v: v.tensor_copy(phic[:, :], phif[:, :]))
            s.em("gpsimd", lambda g, cc=cc: g.dma_start(out=phib_h[cc * 128:(cc + 1) * 128, :], in_=phic[:, :]), dma=True)
        for ti in range(TOK // WL):
            b = ti // 4
            c0 = (ti % 4) * WL
            t0 = ti * WL
            s.em("sync", lambda e, b=b, c0=c0: e.dma_start(
                out=xt[:, :].rearrange("p (c l) -> p c l", c=8), in_=fmj(xsh, b, c0, WL)), dma=True)
            ln.emit(s, xt, xhat)
            s.em("gpsimd", lambda g, t0=t0: g.dma_start(
                out=xhat_h.rearrange("(c p) l -> p c l", p=128)[:, :, t0:t0 + WL],
                in_=xhat[:, :].rearrange("p (c l) -> p c l", c=8)), dma=True)
        for blk_i in range(32):
            bb = blk_i
            s.em("sync", lambda e, bb=bb: e.dma_start_transpose(
                xT[:, bb * 1024:(bb + 1) * 1024], xhat_h[:, bb * 128:(bb + 1) * 128]), dma=True)
        for nch in range(4):
            n0 = nch * 512
            for c in range(8):
                cc = c
                s.em("sync", lambda e, cc=cc, n0=n0: e.dma_start(
                    out=phin[:, cc * 512:(cc + 1) * 512],
                    in_=phib_h[cc * 128:(cc + 1) * 128, n0:n0 + 512]), dma=True)
            for h in range(2):
                for sub in range(4):
                    for b in range(4):
                        t0 = b * 1024 + h * 512 + sub * 128
                        s.em("sync", lambda e, t0=t0: e.dma_start(
                            out=xf[:, :].rearrange("p (c l) -> p c l", c=8),
                            in_=xhat_h.rearrange("(c p) l -> p c l", p=128)[:, :, t0:t0 + 128]), dma=True)
                        pb = b % 2
                        for c in range(8):
                            cc = c
                            s.em("tensor", lambda t, cc=cc, pb=pb: t.matmul(
                                ps_lg[:, pb * 512:(pb + 1) * 512],
                                xf[:, cc * 128:(cc + 1) * 128],
                                phin[:, cc * 512:(cc + 1) * 512],
                                start=(cc == 0), stop=(cc == 7)))
                        s.em("vector", lambda v, b=b, pb=pb: v.tensor_copy(
                            lgs[:, b * 512:(b + 1) * 512], ps_lg[:, pb * 512:(pb + 1) * 512]))
                        s.em("gpsimd", lambda g, t0=t0, n0=n0, b=b: g.dma_start(
                            out=logits_o[t0:t0 + 128, n0:n0 + 512], in_=lgs[:, b * 512:(b + 1) * 512]), dma=True)
                    mx = lrow[:, 0:512]
                    sm = lrow[:, 512:1024]
                    rc = lrow[:, 1024:1536]
                    s.em("vector", lambda v: v.tensor_max(mx, lgs[:, 0:512], lgs[:, 512:1024]))
                    s.em("vector", lambda v: v.tensor_max(sm, lgs[:, 1024:1536], lgs[:, 1536:2048]))
                    s.em("vector", lambda v: v.tensor_max(mx, mx, sm))
                    for b in range(4):
                        s.em("vector", lambda v, b=b: v.tensor_sub(
                            expv[:, b * 512:(b + 1) * 512], lgs[:, b * 512:(b + 1) * 512], mx))
                        s.em("scalar", lambda sc, b=b: sc.activation(
                            expv[:, b * 512:(b + 1) * 512], expv[:, b * 512:(b + 1) * 512], AF.Exp))
                    s.em("vector", lambda v: v.tensor_add(sm, expv[:, 0:512], expv[:, 512:1024]))
                    s.em("vector", lambda v: v.tensor_add(rc, expv[:, 1024:1536], expv[:, 1536:2048]))
                    s.em("vector", lambda v: v.tensor_add(sm, sm, rc))
                    s.em("vector", lambda v: v.reciprocal(rc, sm))
                    s.brk()
                    for b in range(4):
                        di = h * 16 + b * 4 + sub
                        s.em("vector", lambda v, b=b, di=di: v.tensor_mul(
                            disp[:, di * 512: di * 512 + 512],
                            expv[:, b * 512:(b + 1) * 512], rc))
            for b in range(4):
                for dc in range(8):
                    k = 0
                    for h in range(2):
                        for sub in range(4):
                            blk_i = b * 8 + h * 4 + sub
                            di = h * 16 + b * 4 + sub
                            s.em("tensor", lambda t, blk_i=blk_i, dc=dc, di=di, k=k: t.matmul(
                                ps_sl[:, dc * 512:(dc + 1) * 512],
                                xT[:, blk_i * 1024 + dc * 128: blk_i * 1024 + dc * 128 + 128],
                                disp[:, di * 512: di * 512 + 512],
                                start=(k == 0), stop=(k == 7)))
                            k += 1
                for dc in range(8):
                    s.em("vector", lambda v, dc=dc: v.tensor_copy(sevac[:, :], ps_sl[:, dc * 512:(dc + 1) * 512]))
                    s.em("gpsimd", lambda g, b=b, dc=dc, n0=n0: g.dma_start(
                        out=slots_p[b, dc * 128:(dc + 1) * 128, n0:n0 + 512], in_=sevac[:, :]), dma=True)
                    s.brk()

        blk = E(nc.Block())
        s.build(nc, blk, stack)
    return nc



def build_ffnB():
    nc = bass.Bass("TRN2", target_bir_lowering=False)
    slots_in = nc.dram_tensor("slots_in", [8, B, D, 256], F32, kind="ExternalInput").ap()
    W0 = nc.dram_tensor("W0", [8, D, D], F32, kind="ExternalInput").ap()
    b0 = nc.dram_tensor("b0", [8, D], F32, kind="ExternalInput").ap()
    y_o = nc.dram_tensor("y_o", [B, 256, D], BF16, kind="ExternalOutput").ap()

    with ExitStack() as stack:
        E = stack.enter_context
        w0f = E(nc.sbuf_tensor("w0f", [128, D], F32))
        w0b = E(nc.sbuf_tensor("w0b", [128, 8 * D], BF16))
        sl8 = E(nc.sbuf_tensor("sl8", [128, 8 * 256], F32))
        acc = E(nc.sbuf_tensor("acc", [128, 256], F32))
        ssum = E(nc.sbuf_tensor("ssum", [128, 8 * 8 * 4 * 32], BF16))
        b0r = E(nc.sbuf_tensor("b0r", [1, D], F32))
        b0bc = E(nc.sbuf_tensor("b0bc", [128, D], F32))
        onesb = E(nc.sbuf_tensor("onesb", [1, 128], F32))
        ybf = E(nc.sbuf_tensor("ybf", [128, D], BF16))
        ps_y = E(nc.psum_tensor("ps_y", [128, 2 * 512], F32))
        ps_bc = E(nc.psum_tensor("ps_bc", [128, 512], F32))

        ssum_v = ssum[:, :].rearrange("p (dc n b q) -> p dc n b q", dc=8, n=8, b=4)

        s = Seq()
        s.em("vector", lambda v: v.memset(onesb[:, :], 1.0))
        for b in range(4):
            for dc in range(8):
                for src in range(8):
                    s.em("sync", lambda e, b=b, dc=dc, src=src: e.dma_start(
                        out=sl8[:, src * 256:(src + 1) * 256],
                        in_=slots_in[src, b, dc * 128:(dc + 1) * 128, :]), dma=True)
                s.em("vector", lambda v: v.tensor_add(acc[:, :], sl8[:, 0:256], sl8[:, 256:512]))
                for src in range(2, 8):
                    s.em("vector", lambda v, src=src: v.tensor_add(
                        acc[:, :], acc[:, :], sl8[:, src * 256:(src + 1) * 256]))
                s.em("vector", lambda v, b=b, dc=dc: v.tensor_copy(
                    ssum_v[:, dc, :, b, :],
                    acc[:, :].rearrange("p (n q) -> p n q", n=8)))
                s.brk()
        for n in range(8):
            for dc in range(8):
                s.em("sync", lambda e, n=n, dc=dc: e.dma_start(
                    out=w0f[:, :], in_=W0[n, dc * 128:(dc + 1) * 128, :]), dma=True)
                s.brk()
                s.em("vector", lambda v, dc=dc: v.tensor_copy(w0b[:, dc * D:(dc + 1) * D], w0f[:, :]))
                s.brk()
            s.em("sync", lambda e, n=n: e.dma_start(out=b0r[:, :], in_=b0[n:n + 1, :]), dma=True)
            for eh in range(2):
                s.em("tensor", lambda t, eh=eh: t.matmul(
                    ps_bc[:, :], onesb[:, :], b0r[:, eh * 512:(eh + 1) * 512], start=True, stop=True))
                s.em("vector", lambda v, eh=eh: v.tensor_copy(b0bc[:, eh * 512:(eh + 1) * 512], ps_bc[:, :]))
                s.brk()
            for eh in range(2):
                for dc in range(8):
                    s.em("tensor", lambda t, n=n, dc=dc, eh=eh: t.matmul(
                        ps_y[:, eh * 512:(eh + 1) * 512],
                        ssum[:, (dc * 8 + n) * 128: (dc * 8 + n) * 128 + 128],
                        w0b[:, dc * D + eh * 512: dc * D + eh * 512 + 512],
                        start=(dc == 0), stop=(dc == 7)))
                s.em("vector", lambda v, eh=eh: v.tensor_add(
                    ybf[:, eh * 512:(eh + 1) * 512], ps_y[:, eh * 512:(eh + 1) * 512],
                    b0bc[:, eh * 512:(eh + 1) * 512]))
            for b in range(4):
                s.em("gpsimd", lambda g, n=n, b=b: g.dma_start(
                    out=y_o[b, n * 32:(n + 1) * 32, :], in_=ybf[b * 32:(b + 1) * 32, :]), dma=True)
            s.brk()

        blk = E(nc.Block())
        s.build(nc, blk, stack)
    return nc



def build_ffnC():
    nc = bass.Bass("TRN2", target_bir_lowering=False)
    NP = 2048
    xsh = nc.dram_tensor("xsh", [B, D, LC], F32, kind="ExternalInput").ap()
    logits = nc.dram_tensor("logits", [TOK, NP], F32, kind="ExternalInput").ap()
    y_all = nc.dram_tensor("y_all", [B, NP, D], BF16, kind="ExternalInput").ap()
    iden = nc.dram_tensor("iden", [128, 128], F32, kind="ExternalInput").ap()
    xout = nc.dram_tensor("xout", [B, D, LC], F32, kind="ExternalOutput").ap()

    W = 512
    with ExitStack() as stack:
        E = stack.enter_context
        idf = E(nc.sbuf_tensor("idf", [128, 128], F32))
        idb = E(nc.sbuf_tensor("idb", [128, 128], BF16))
        ysb = E(nc.sbuf_tensor("ysb", [128, 16 * D], BF16))
        lg = E(nc.sbuf_tensor("lg", [128, NP], F32))
        ex = E(nc.sbuf_tensor("ex", [128, NP], F32))
        cb = E(nc.sbuf_tensor("cb", [128, NP], BF16))
        cT = E(nc.sbuf_tensor("cT", [128, 16 * W], BF16))
        cols = E(nc.sbuf_tensor("cols", [128, 4], F32))
        xt = E(nc.sbuf_tensor("xt", [128, 8 * W], F32))
        ot = E(nc.sbuf_tensor("ot", [128, 8 * W], F32))
        ps_t = E(nc.psum_tensor("ps_t", [128, 128], BF16))
        ps_o = E(nc.psum_tensor("ps_o", [128, 4 * W], F32))

        s = Seq()
        s.em("sync", lambda e: e.dma_start(out=idf[:, :], in_=iden[:, :]), dma=True)
        s.em("vector", lambda v: v.tensor_copy(idb[:, :], idf[:, :]))
        for b in range(4):
            for npc in range(16):
                s.em("sync", lambda e, b=b, npc=npc: e.dma_start(
                    out=ysb[:, npc * D:(npc + 1) * D], in_=y_all[b, npc * 128:(npc + 1) * 128, :]), dma=True)
            for h in range(2):
                t0 = b * 1024 + h * 512
                for sub in range(4):
                    ts = t0 + sub * 128
                    s.em("sync", lambda e, ts=ts: e.dma_start(
                        out=lg[:, :], in_=logits[ts:ts + 128, :]), dma=True)
                    mx = cols[:, 0:1]
                    s.brk()
                    s.em("vector", lambda v: v.reduce_max(mx, lg[:, :], axis=AX.X))
                    s.brk()
                    s.em("vector", lambda v: v.tensor_scalar_mul(cols[:, 1:2], mx, -1.0))
                    s.brk()
                    s.em("scalar", lambda sc: sc.activation(ex[:, :], lg[:, :], AF.Exp, bias=cols[:, 1:2]))
                    s.brk()
                    s.em("vector", lambda v: v.reduce_sum(cols[:, 2:3], ex[:, :], axis=AX.X))
                    s.brk()
                    s.em("vector", lambda v: v.reciprocal(cols[:, 3:4], cols[:, 2:3]))
                    s.brk()
                    s.em("vector", lambda v: v.tensor_scalar_mul(cb[:, :], ex[:, :], cols[:, 3:4]))
                    for npc in range(16):
                        s.em("tensor", lambda t, npc=npc: t.transpose(
                            ps_t[:, :], cb[:, npc * 128:(npc + 1) * 128], idb[:, :]))
                        s.brk()
                        s.em("vector", lambda v, npc=npc, sub=sub: v.tensor_copy(
                            cT[:, npc * W + sub * 128: npc * W + sub * 128 + 128], ps_t[:, :]))
                        s.brk()
                for dc in range(8):
                    pse = dc % 4
                    for npc in range(16):
                        s.em("tensor", lambda t, dc=dc, npc=npc, pse=pse: t.matmul(
                            ps_o[:, pse * W:(pse + 1) * W],
                            ysb[:, npc * D + dc * 128: npc * D + dc * 128 + 128],
                            cT[:, npc * W:(npc + 1) * W],
                            start=(npc == 0), stop=(npc == 15)))
                    if pse == 3 or dc == 7:
                        pass
                    if dc % 4 == 3:
                        c0 = h * W
                        s.em("sync", lambda e, b=b, c0=c0: e.dma_start(
                            out=xt[:, :].rearrange("p (c l) -> p c l", c=8), in_=fmj(xsh, b, c0, W)), dma=True) if dc == 3 else None
                        for q in range(4):
                            dcq = dc - 3 + q
                            s.em("vector", lambda v, dcq=dcq, q=q: v.scalar_tensor_tensor(
                                ot[:, dcq * W:(dcq + 1) * W], ps_o[:, q * W:(q + 1) * W], 0.5,
                                xt[:, dcq * W:(dcq + 1) * W], ALU.mult, ALU.add))
                s.em("gpsimd", lambda g, b=b, h=h: g.dma_start(
                    out=fmj(xout, b, h * W, W), in_=ot[:, :].rearrange("p (c l) -> p c l", c=8)), dma=True)

        blk = E(nc.Block())
        s.build(nc, blk, stack)
    return nc


def run_l0ffn(x_fm, phi2d, W0, b0):
    p1 = get_prog("ffnA", build_ffnA)
    r1 = run_bass_kernel_spmd(
        p1, [dict(xsh=x_fm[c], phi=phi2d) for c in range(NC)], core_ids=list(range(NC)))
    slots_p = [r1.results[c]["slots_p"] for c in range(NC)]
    logits = [r1.results[c]["logits_o"] for c in range(NC)]
    p2 = get_prog("ffnB", build_ffnB)
    in2 = []
    for c in range(NC):
        sl = np.stack([sp[:, :, c * 256:(c + 1) * 256] for sp in slots_p], 0)
        in2.append(dict(slots_in=np.ascontiguousarray(sl),
                        W0=W0[c * 8:(c + 1) * 8], b0=b0[c * 8:(c + 1) * 8]))
    r2 = run_bass_kernel_spmd(p2, in2, core_ids=list(range(NC)))
    y_all = np.concatenate([r2.results[c]["y_o"] for c in range(NC)], axis=1)
    iden = np.eye(128, dtype=np.float32)
    p3 = get_prog("ffnC", build_ffnC)
    in3 = [dict(xsh=x_fm[c], logits=logits[c], y_all=y_all, iden=iden) for c in range(NC)]
    r3 = run_bass_kernel_spmd(p3, in3, core_ids=list(range(NC)))
    return [r3.results[c]["xout"] for c in range(NC)]



def build_mamba1():
    nc = bass.Bass("TRN2", target_bir_lowering=False)
    xsh = nc.dram_tensor("xsh", [B, D, LC], F32, kind="ExternalInput").ap()
    xb3 = nc.dram_tensor("xb3", [B, D, 3], F32, kind="ExternalInput").ap()
    W_inT = nc.dram_tensor("W_inT", [D, 2048], F32, kind="ExternalInput").ap()
    Wc4 = nc.dram_tensor("Wc4", [D, 4], F32, kind="ExternalInput").ap()
    bconv = nc.dram_tensor("bconv", [D], F32, kind="ExternalInput").ap()
    W_dtT = nc.dram_tensor("W_dtT", [D, D], F32, kind="ExternalInput").ap()
    bdt = nc.dram_tensor("bdt", [D], F32, kind="ExternalInput").ap()
    W_xT = nc.dram_tensor("W_xT", [D, 32], F32, kind="ExternalInput").ap()
    delta_o = nc.dram_tensor("delta_o", [B, D, LC], F32, kind="ExternalOutput").ap()
    gate_o = nc.dram_tensor("gate_o", [B, D, LC], BF16, kind="ExternalOutput").ap()
    bc_o = nc.dram_tensor("bc_o", [B, 32, LC], F32, kind="ExternalOutput").ap()

    W = 512
    XW = LC + 3
    with ExitStack() as stack:
        E = stack.enter_context
        wst = E(nc.sbuf_tensor("wst", [128, 2048], F32))
        winb = E(nc.sbuf_tensor("winb", [128, 8 * 2048], BF16))
        wdtb = E(nc.sbuf_tensor("wdtb", [128, 8 * D], BF16))
        wxb = E(nc.sbuf_tensor("wxb", [128, 8 * 32], BF16))
        wcc = E(nc.sbuf_tensor("wcc", [128, 8 * 4], F32))
        bcc = E(nc.sbuf_tensor("bcc", [128, 8], F32))
        bdc = E(nc.sbuf_tensor("bdc", [128, 8], F32))
        xt = E(nc.sbuf_tensor("xt", [128, 8 * W], F32))
        xhat = E(nc.sbuf_tensor("xhat", [128, 8 * W], BF16))
        x3t = E(nc.sbuf_tensor("x3t", [128, 8 * 3], F32))
        x3h = E(nc.sbuf_tensor("x3h", [128, 8 * 3], BF16))
        r3 = E(nc.sbuf_tensor("r3", [1, 8 * 3], F32))
        xin = E(nc.sbuf_tensor("xin", [128, 8 * XW], F32))
        ztile = E(nc.sbuf_tensor("ztile", [128, 8 * 512], F32))
        gtile = E(nc.sbuf_tensor("gtile", [128, 8 * 512], BF16))
        cv = E(nc.sbuf_tensor("cv", [128, 512], F32))
        cv2 = E(nc.sbuf_tensor("cv2", [128, 512], F32))
        sg = E(nc.sbuf_tensor("sg", [128, 512], F32))
        dt1 = E(nc.sbuf_tensor("dt1", [128, 512], F32))
        dt2 = E(nc.sbuf_tensor("dt2", [128, 512], F32))
        bce = E(nc.sbuf_tensor("bce", [32, 512], F32))
        ps_all = E(nc.psum_tensor("ps_all", [128, 8 * 512], F32))
        ln = LNKit(nc, E, W, bank=ps_all[:, 2048:2048 + 2 * W])
        ps_a = ps_all[:, 0:512]
        ps_b = ps_all[:, 512:1024]
        ps_bc3 = ps_all[0:32, 1024:1536]

        s = Seq()
        ln.emit_init(s)
        for c in range(8):
            cc = c
            s.em("sync", lambda e, cc=cc: e.dma_start(out=wst[:, :], in_=W_inT[cc * 128:(cc + 1) * 128, :]), dma=True)
            s.brk()
            s.em("vector", lambda v, cc=cc: v.tensor_copy(winb[:, cc * 2048:(cc + 1) * 2048], wst[:, :]))
            s.brk()
        for c in range(8):
            cc = c
            s.em("sync", lambda e, cc=cc: e.dma_start(out=wst[:, 0:D], in_=W_dtT[cc * 128:(cc + 1) * 128, :]), dma=True)
            s.brk()
            s.em("vector", lambda v, cc=cc: v.tensor_copy(wdtb[:, cc * D:(cc + 1) * D], wst[:, 0:D]))
            s.brk()
        for c in range(8):
            cc = c
            s.em("sync", lambda e, cc=cc: e.dma_start(out=wst[:, 0:32], in_=W_xT[cc * 128:(cc + 1) * 128, :]), dma=True)
            s.brk()
            s.em("vector", lambda v, cc=cc: v.tensor_copy(wxb[:, cc * 32:(cc + 1) * 32], wst[:, 0:32]))
            s.brk()
        s.em("sync", lambda e: e.dma_start(out=wcc[:, :].rearrange("p (c j) -> p c j", c=8),
             in_=Wc4.rearrange("(c p) j -> p c j", p=128)), dma=True)
        s.em("sync", lambda e: e.dma_start(out=bcc[:, :], in_=bconv.rearrange("(c p) -> p c", p=128),
             allow_slow_non_contiguous=True), dma=True)
        s.em("sync", lambda e: e.dma_start(out=bdc[:, :], in_=bdt.rearrange("(c p) -> p c", p=128),
             allow_slow_non_contiguous=True), dma=True)

        for b in range(B):
            s.em("sync", lambda e, b=b: e.dma_start(
                out=x3t[:, :].rearrange("p (c l) -> p c l", c=8),
                in_=xb3.rearrange("b (c p) l -> b p c l", p=128)[b]), dma=True)
            s.em("vector", lambda v: v.tensor_copy(x3h[:, :], x3t[:, :]))
            for c in range(8):
                cc = c
                s.em("tensor", lambda t, cc=cc: t.matmul(
                    ps_all[0:1, 2048:2051], ln.ones1[:, :],
                    x3h[:, cc * 3:(cc + 1) * 3], start=(cc == 0), stop=(cc == 7)))
            s.em("scalar", lambda sc: sc.activation(x3h[:, :], x3t[:, :], AF.Square))
            s.brk()
            for c in range(8):
                cc = c
                s.em("tensor", lambda t, cc=cc: t.matmul(
                    ps_all[0:1, 2560:2563], ln.ones1[:, :],
                    x3h[:, cc * 3:(cc + 1) * 3], start=(cc == 0), stop=(cc == 7)))
            mu3 = r3[:, 0:3]; msq3 = r3[:, 3:6]; var3 = r3[:, 6:9]; rst3 = r3[:, 9:12]; rr3 = r3[:, 12:15]; tp3 = r3[:, 15:18]
            s.em("vector", lambda v: v.tensor_scalar_mul(mu3, ps_all[0:1, 2048:2051], 1.0 / D))
            s.em("vector", lambda v: v.tensor_scalar_mul(msq3, ps_all[0:1, 2560:2563], 1.0 / D))
            s.brk()
            s.em("vector", lambda v: v.tensor_mul(tp3, mu3, mu3))
            s.brk()
            s.em("vector", lambda v: v.tensor_sub(var3, msq3, tp3))
            s.brk()
            s.em("vector", lambda v: v.tensor_scalar_add(var3, var3, EPS))
            s.brk()
            s.em("scalar", lambda sc: sc.activation(var3, var3, AF.Sqrt))
            s.brk()
            s.em("vector", lambda v: v.reciprocal(rst3, var3))
            s.brk()
            s.em("vector", lambda v: v.tensor_mul(tp3, mu3, rst3))
            s.brk()
            s.em("vector", lambda v: v.tensor_scalar_mul(rr3, tp3, -1.0))
            s.brk()
            s.em("tensor", lambda t: t.matmul(ps_a[:, 0:3], ln.onesb[:, :], rst3, start=True, stop=True))
            s.em("tensor", lambda t: t.matmul(ps_b[:, 0:3], ln.onesb[:, :], rr3, start=True, stop=True))
            for c in range(8):
                cc = c
                sl = slice(cc * 3, (cc + 1) * 3)
                s.em("vector", lambda v, sl=sl: v.tensor_mul(x3t[:, sl], x3t[:, sl], ps_a[:, 0:3]))
                s.brk()
                s.em("vector", lambda v, sl=sl: v.tensor_add(x3h[:, sl], x3t[:, sl], ps_b[:, 0:3]))
                s.brk()
            for ic in range(8):
                icc = ic
                for c in range(8):
                    cc = c
                    s.em("tensor", lambda t, icc=icc, cc=cc: t.matmul(
                        ps_a[:, 0:3],
                        winb[:, cc * 2048 + icc * 128: cc * 2048 + icc * 128 + 128],
                        x3h[:, cc * 3:(cc + 1) * 3],
                        start=(cc == 0), stop=(cc == 7)))
                s.em("vector", lambda v, icc=icc: v.tensor_copy(
                    xin[:, icc * XW: icc * XW + 3], ps_a[:, 0:3]))
                s.brk()
            for h in range(2):
                c0 = h * W
                s.em("sync", lambda e, b=b, c0=c0: e.dma_start(
                    out=xt[:, :].rearrange("p (c l) -> p c l", c=8), in_=fmj(xsh, b, c0, W)), dma=True)
                ln.emit(s, xt, xhat)
                for ic in range(16):
                    icc = ic
                    pb = ps_a if ic % 2 == 0 else ps_b
                    for c in range(8):
                        cc = c
                        s.em("tensor", lambda t, icc=icc, cc=cc, pb=pb: t.matmul(
                            pb[:, :],
                            winb[:, cc * 2048 + icc * 128: cc * 2048 + icc * 128 + 128],
                            xhat[:, cc * W:(cc + 1) * W],
                            start=(cc == 0), stop=(cc == 7)))
                    if icc < 8:
                        s.em("vector", lambda v, icc=icc, h=h, pb=pb: v.tensor_copy(
                            xin[:, icc * XW + 3 + h * W: icc * XW + 3 + h * W + W], pb[:, :]))
                    else:
                        s.em("vector", lambda v, icc=icc, pb=pb: v.tensor_copy(
                            ztile[:, (icc - 8) * W: (icc - 8) * W + W], pb[:, :]))
                    s.brk()
                for c in range(8):
                    cc = c
                    base = cc * XW + h * W
                    s.em("vector", lambda v, base=base, cc=cc: v.tensor_scalar_mul(
                        cv[:, :], xin[:, base: base + W], wcc[:, cc * 4:cc * 4 + 1]))
                    s.em("vector", lambda v, base=base, cc=cc: v.scalar_tensor_tensor(
                        cv2[:, :], xin[:, base + 1: base + 1 + W], wcc[:, cc * 4 + 1:cc * 4 + 2],
                        cv[:, :], ALU.mult, ALU.add))
                    s.em("vector", lambda v, base=base, cc=cc: v.scalar_tensor_tensor(
                        cv[:, :], xin[:, base + 2: base + 2 + W], wcc[:, cc * 4 + 2:cc * 4 + 3],
                        cv2[:, :], ALU.mult, ALU.add))
                    s.em("vector", lambda v, base=base, cc=cc: v.scalar_tensor_tensor(
                        cv2[:, :], xin[:, base + 3: base + 3 + W], wcc[:, cc * 4 + 3:cc * 4 + 4],
                        cv[:, :], ALU.mult, ALU.add))
                    s.em("scalar", lambda sc, cc=cc: sc.activation(
                        sg[:, :], cv2[:, :], AF.Silu, bias=bcc[:, cc:cc + 1]))
                    s.brk()
                    s.em("vector", lambda v, cc=cc: v.tensor_mul(
                        cv[:, :], sg[:, :], ztile[:, cc * W:(cc + 1) * W]))
                    s.em("vector", lambda v, cc=cc: v.tensor_copy(gtile[:, cc * W:(cc + 1) * W], cv[:, :]))
                    s.em("gpsimd", lambda g, b=b, cc=cc, c0=c0: g.dma_start(
                        out=gate_o[b, cc * 128:(cc + 1) * 128, c0:c0 + W],
                        in_=gtile[:, cc * W:(cc + 1) * W]), dma=True)
                for dc in range(8):
                    dcc = dc
                    pb = ps_a if dc % 2 == 0 else ps_b
                    for c in range(8):
                        cc = c
                        s.em("tensor", lambda t, dcc=dcc, cc=cc, pb=pb: t.matmul(
                            pb[:, :],
                            wdtb[:, cc * D + dcc * 128: cc * D + dcc * 128 + 128],
                            gtile[:, cc * W:(cc + 1) * W],
                            start=(cc == 0), stop=(cc == 7)))
                    s.em("scalar", lambda sc, dcc=dcc, pb=pb: sc.activation(
                        dt1[:, :], pb[:, :], AF.Abs, bias=bdc[:, dcc:dcc + 1]))
                    s.em("scalar", lambda sc: sc.activation(dt1[:, :], dt1[:, :], AF.Exp, scale=-1.0))
                    s.em("scalar", lambda sc: sc.activation(dt1[:, :], dt1[:, :], AF.Ln, bias=1.0))
                    s.em("scalar", lambda sc, dcc=dcc, pb=pb: sc.activation(
                        dt2[:, :], pb[:, :], AF.Relu, bias=bdc[:, dcc:dcc + 1]))
                    s.em("vector", lambda v: v.tensor_add(dt1[:, :], dt1[:, :], dt2[:, :]))
                    s.em("gpsimd", lambda g, b=b, dcc=dcc, c0=c0: g.dma_start(
                        out=delta_o[b, dcc * 128:(dcc + 1) * 128, c0:c0 + W], in_=dt1[:, :]), dma=True)
                    s.brk()
                for c in range(8):
                    cc = c
                    s.em("tensor", lambda t, cc=cc: t.matmul(
                        ps_bc3[:, 0:W],
                        wxb[:, cc * 32:(cc + 1) * 32],
                        gtile[:, cc * W:(cc + 1) * W],
                        start=(cc == 0), stop=(cc == 7)))
                s.em("vector", lambda v: v.tensor_copy(bce[:, :], ps_bc3[:, 0:W]))
                s.em("gpsimd", lambda g, b=b, c0=c0: g.dma_start(
                    out=bc_o[b, :, c0:c0 + W], in_=bce[:, :]), dma=True)
                s.brk()

        blk = E(nc.Block())
        s.build(nc, blk, stack)
    return nc



def build_mamba2():
    nc = bass.Bass("TRN2", target_bir_lowering=False)
    LT = NC * LC
    delta_own = nc.dram_tensor("delta_own", [512, LT], F32, kind="ExternalInput").ap()
    gate_own = nc.dram_tensor("gate_own", [512, LT], BF16, kind="ExternalInput").ap()
    Bn = nc.dram_tensor("Bn", [16, LT], F32, kind="ExternalInput").ap()
    Cn = nc.dram_tensor("Cn", [16, LT], F32, kind="ExternalInput").ap()
    A_own = nc.dram_tensor("A_own", [512, 16], F32, kind="ExternalInput").ap()
    y_own = nc.dram_tensor("y_own", [512, LT], F32, kind="ExternalOutput").ap()

    H = 4096
    with ExitStack() as stack:
        E = stack.enter_context
        dl = E(nc.sbuf_tensor("dl", [128, H], F32))
        gt = E(nc.sbuf_tensor("gt", [128, H], BF16))
        dA = E(nc.sbuf_tensor("dA", [128, H], F32))
        tmp = E(nc.sbuf_tensor("tmp", [128, H], F32))
        dBx = E(nc.sbuf_tensor("dBx", [128, H], F32))
        hout = E(nc.sbuf_tensor("hout", [128, H], F32))
        Bbc = E(nc.sbuf_tensor("Bbc", [128, H], F32))
        Cbc = E(nc.sbuf_tensor("Cbc", [128, H], F32))
        yac = E(nc.sbuf_tensor("yac", [128, H], F32))
        tmp2 = E(nc.sbuf_tensor("tmp2", [128, H], F32))
        Asb = E(nc.sbuf_tensor("Asb", [128, 4 * 16], F32))
        stc = E(nc.sbuf_tensor("stc", [128, 16], F32))

        s = Seq()
        s.em("sync", lambda e: e.dma_start(
            out=Asb[:, :].rearrange("p (c n) -> p c n", c=4),
            in_=A_own.rearrange("(c p) n -> p c n", p=128)), dma=True)
        s.em("scalar", lambda sc: sc.activation(Asb[:, :], Asb[:, :], AF.Exp))
        s.brk()
        s.em("vector", lambda v: v.tensor_scalar_mul(Asb[:, :], Asb[:, :], -1.0))
        s.brk()
        for ds in range(4):
            s.em("vector", lambda v: v.memset(stc[:, :], 0.0))
            for hf in range(2):
                l0 = hf * H
                s.em("sync", lambda e, ds=ds, l0=l0: e.dma_start(
                    out=dl[:, :], in_=delta_own[ds * 128:(ds + 1) * 128, l0:l0 + H]), dma=True)
                s.em("sync", lambda e, ds=ds, l0=l0: e.dma_start(
                    out=gt[:, :], in_=gate_own[ds * 128:(ds + 1) * 128, l0:l0 + H]), dma=True)
                for n in range(16):
                    nn = n
                    s.em("sync", lambda e, nn=nn, l0=l0: e.dma_start(
                        out=Bbc[:, :], in_=Bn[nn:nn + 1, l0:l0 + H].broadcast_to((128, H))), dma=True)
                    s.em("sync", lambda e, nn=nn, l0=l0: e.dma_start(
                        out=Cbc[:, :], in_=Cn[nn:nn + 1, l0:l0 + H].broadcast_to((128, H))), dma=True)
                    s.em("scalar", lambda sc, ds=ds, nn=nn: sc.activation(
                        dA[:, :], dl[:, :], AF.Exp, scale=Asb[:, ds * 16 + nn: ds * 16 + nn + 1]))
                    s.em("vector", lambda v: v.tensor_mul(tmp[:, :], dl[:, :], Bbc[:, :]))
                    s.em("vector", lambda v: v.tensor_mul(dBx[:, :], tmp[:, :], gt[:, :]))
                    s.em("vector", lambda v, nn=nn: v.tensor_tensor_scan(
                        hout[:, :], dA[:, :], dBx[:, :], stc[:, nn:nn + 1], ALU.mult, ALU.add))
                    s.brk()
                    s.em("vector", lambda v, nn=nn: v.tensor_copy(stc[:, nn:nn + 1], hout[:, H - 1:H]))
                    s.brk()
                    if n == 0:
                        s.em("vector", lambda v: v.tensor_mul(yac[:, :], hout[:, :], Cbc[:, :]))
                    else:
                        s.em("vector", lambda v: v.tensor_mul(tmp2[:, :], hout[:, :], Cbc[:, :]))
                        s.em("vector", lambda v: v.tensor_add(yac[:, :], yac[:, :], tmp2[:, :]))
                s.em("gpsimd", lambda g, ds=ds, l0=l0: g.dma_start(
                    out=y_own[ds * 128:(ds + 1) * 128, l0:l0 + H], in_=yac[:, :]), dma=True)
                s.brk()

        blk = E(nc.Block())
        s.build(nc, blk, stack)
    return nc



def build_mamba3():
    nc = bass.Bass("TRN2", target_bir_lowering=False)
    xsh = nc.dram_tensor("xsh", [B, D, LC], F32, kind="ExternalInput").ap()
    y_in = nc.dram_tensor("y_in", [B, D, LC], F32, kind="ExternalInput").ap()
    lng = nc.dram_tensor("lng", [D], F32, kind="ExternalInput").ap()
    lnb = nc.dram_tensor("lnb", [D], F32, kind="ExternalInput").ap()
    W_outT = nc.dram_tensor("W_outT", [D, D], F32, kind="ExternalInput").ap()
    bout = nc.dram_tensor("bout", [D], F32, kind="ExternalInput").ap()
    xout = nc.dram_tensor("xout", [B, D, LC], F32, kind="ExternalOutput").ap()

    W = 512
    with ExitStack() as stack:
        E = stack.enter_context
        wst = E(nc.sbuf_tensor("wst", [128, D], F32))
        wob = E(nc.sbuf_tensor("wob", [128, 8 * D], BF16))
        gcol = E(nc.sbuf_tensor("gcol", [128, 8], F32))
        bcol = E(nc.sbuf_tensor("bcol", [128, 8], F32))
        ocol = E(nc.sbuf_tensor("ocol", [128, 8], F32))
        xt = E(nc.sbuf_tensor("xt", [128, 8 * W], F32))
        yt = E(nc.sbuf_tensor("yt", [128, 8 * W], F32))
        yr = E(nc.sbuf_tensor("yr", [128, 8 * W], F32))
        yh = E(nc.sbuf_tensor("yh", [128, 8 * W], BF16))
        ot = E(nc.sbuf_tensor("ot", [128, 8 * W], F32))
        ps_all = E(nc.psum_tensor("ps_all", [128, 8 * 512], F32))
        ln = LNKit(nc, E, W, bank=ps_all[:, 2048:2048 + 2 * W])
        ps_a = ps_all[:, 0:512]
        ps_b = ps_all[:, 512:1024]

        s = Seq()
        ln.emit_init(s)
        for c in range(8):
            cc = c
            s.em("sync", lambda e, cc=cc: e.dma_start(out=wst[:, :], in_=W_outT[cc * 128:(cc + 1) * 128, :]), dma=True)
            s.brk()
            s.em("vector", lambda v, cc=cc: v.tensor_copy(wob[:, cc * D:(cc + 1) * D], wst[:, :]))
            s.brk()
        s.em("sync", lambda e: e.dma_start(out=gcol[:, :], in_=lng.rearrange("(c p) -> p c", p=128),
             allow_slow_non_contiguous=True), dma=True)
        s.em("sync", lambda e: e.dma_start(out=bcol[:, :], in_=lnb.rearrange("(c p) -> p c", p=128),
             allow_slow_non_contiguous=True), dma=True)
        s.em("sync", lambda e: e.dma_start(out=ocol[:, :], in_=bout.rearrange("(c p) -> p c", p=128),
             allow_slow_non_contiguous=True), dma=True)

        for ti in range(TOK // W):
            b = ti // 2
            c0 = (ti % 2) * W
            s.em("sync", lambda e, b=b, c0=c0: e.dma_start(
                out=xt[:, :].rearrange("p (c l) -> p c l", c=8), in_=fmj(xsh, b, c0, W)), dma=True)
            s.em("sync", lambda e, b=b, c0=c0: e.dma_start(
                out=yt[:, :].rearrange("p (c l) -> p c l", c=8), in_=fmj(y_in, b, c0, W)), dma=True)
            s.em("vector", lambda v: v.tensor_add(yr[:, :], yt[:, :], xt[:, :]))
            ln.emit(s, yr, yh, gam_col=lambda cc: gcol[:, cc:cc + 1], bet_col=lambda cc: bcol[:, cc:cc + 1])
            for dc in range(8):
                dcc = dc
                pb = ps_a if dc % 2 == 0 else ps_b
                for c in range(8):
                    cc = c
                    s.em("tensor", lambda t, dcc=dcc, cc=cc, pb=pb: t.matmul(
                        pb[:, :],
                        wob[:, cc * D + dcc * 128: cc * D + dcc * 128 + 128],
                        yh[:, cc * W:(cc + 1) * W],
                        start=(cc == 0), stop=(cc == 7)))
                s.em("vector", lambda v, dcc=dcc, pb=pb: v.scalar_tensor_tensor(
                    ot[:, dcc * W:(dcc + 1) * W], pb[:, :], ocol[:, dcc:dcc + 1],
                    xt[:, dcc * W:(dcc + 1) * W], ALU.add, ALU.add))
                s.brk()
            s.em("gpsimd", lambda g, b=b, c0=c0: g.dma_start(
                out=fmj(xout, b, c0, W), in_=ot[:, :].rearrange("p (c l) -> p c l", c=8)), dma=True)

        blk = E(nc.Block())
        s.build(nc, blk, stack)
    return nc


def run_mamba(x_fm, W_in, W_conv, b_conv, W_xproj, W_dt, b_dt, A_log, ln_g, ln_b, W_out, b_out):
    W_inT = np.ascontiguousarray(W_in.T)
    Wc4 = np.ascontiguousarray(W_conv[:, 0, :])
    W_dtT = np.ascontiguousarray(W_dt.T)
    W_xT = np.ascontiguousarray(W_xproj.T)
    W_outT = np.ascontiguousarray(W_out.T)
    p4 = get_prog("mamba1", build_mamba1)
    in4 = []
    for c in range(NC):
        if c == 0:
            xb3 = np.zeros((B, D, 3), np.float32)
        else:
            xb3 = np.ascontiguousarray(x_fm[c - 1][:, :, LC - 3:LC])
        in4.append(dict(xsh=x_fm[c], xb3=xb3, W_inT=W_inT, Wc4=Wc4, bconv=b_conv,
                        W_dtT=W_dtT, bdt=b_dt, W_xT=W_xT))
    r4 = run_bass_kernel_spmd(p4, in4, core_ids=list(range(NC)))
    delta = [r4.results[c]["delta_o"] for c in range(NC)]
    gate = [r4.results[c]["gate_o"] for c in range(NC)]
    bc = [r4.results[c]["bc_o"] for c in range(NC)]
    p5 = get_prog("mamba2", build_mamba2)
    in5 = []
    for c in range(NC):
        b = c // 2
        dh = c % 2
        dsl = slice(dh * 512, (dh + 1) * 512)
        in5.append(dict(
            delta_own=np.ascontiguousarray(np.concatenate([delta[k][b][dsl] for k in range(NC)], axis=1)),
            gate_own=np.ascontiguousarray(np.concatenate([gate[k][b][dsl] for k in range(NC)], axis=1)),
            Bn=np.ascontiguousarray(np.concatenate([bc[k][b][0:16] for k in range(NC)], axis=1)),
            Cn=np.ascontiguousarray(np.concatenate([bc[k][b][16:32] for k in range(NC)], axis=1)),
            A_own=np.ascontiguousarray(A_log[dsl])))
    r5 = run_bass_kernel_spmd(p5, in5, core_ids=list(range(NC)))
    yown = [r5.results[c]["y_own"] for c in range(NC)]
    p6 = get_prog("mamba3", build_mamba3)
    in6 = []
    for c in range(NC):
        y = np.empty((B, D, LC), np.float32)
        for b in range(B):
            for dh in range(2):
                y[b, dh * 512:(dh + 1) * 512, :] = yown[b * 2 + dh][:, c * LC:(c + 1) * LC]
        in6.append(dict(xsh=x_fm[c], y_in=y, lng=ln_g, lnb=ln_b, W_outT=W_outT, bout=b_out))
    r6 = run_bass_kernel_spmd(p6, in6, core_ids=list(range(NC)))
    return [r6.results[c]["xout"] for c in range(NC)]



def build_wkv1():
    nc = bass.Bass("TRN2", target_bir_lowering=False)
    xsh = nc.dram_tensor("xsh", [B, D, LC], F32, kind="ExternalInput").ap()
    xb1 = nc.dram_tensor("xb1", [B, D, 3], F32, kind="ExternalInput").ap()
    lam = nc.dram_tensor("lam", [6, D], F32, kind="ExternalInput").ap()
    x_A = nc.dram_tensor("x_A", [D, 192], F32, kind="ExternalInput").ap()
    x_B2 = nc.dram_tensor("x_B2", [6, 32, D], F32, kind="ExternalInput").ap()
    xmiu = nc.dram_tensor("xmiu", [D], F32, kind="ExternalInput").ap()
    xtime = nc.dram_tensor("xtime", [D], F32, kind="ExternalInput").ap()
    k_A = nc.dram_tensor("k_A", [D, 64], F32, kind="ExternalInput").ap()
    k_B = nc.dram_tensor("k_B", [64, D], F32, kind="ExternalInput").ap()
    g_A = nc.dram_tensor("g_A", [D, 128], F32, kind="ExternalInput").ap()
    g_B = nc.dram_tensor("g_B", [128, D], F32, kind="ExternalInput").ap()
    w_A = nc.dram_tensor("w_A", [D, 64], F32, kind="ExternalInput").ap()
    w_B = nc.dram_tensor("w_B", [64, D], F32, kind="ExternalInput").ap()
    wmiu = nc.dram_tensor("wmiu", [D], F32, kind="ExternalInput").ap()
    a_A = nc.dram_tensor("a_A", [D, 64], F32, kind="ExternalInput").ap()
    a_B = nc.dram_tensor("a_B", [64, D], F32, kind="ExternalInput").ap()
    atime = nc.dram_tensor("atime", [D], F32, kind="ExternalInput").ap()
    rWT = nc.dram_tensor("rWT", [D, D], F32, kind="ExternalInput").ap()
    kWT = nc.dram_tensor("kWT", [D, D], F32, kind="ExternalInput").ap()
    vWT = nc.dram_tensor("vWT", [D, D], F32, kind="ExternalInput").ap()
    hsel = nc.dram_tensor("hsel", [8, 128, 16], F32, kind="ExternalInput").ap()
    hbc = nc.dram_tensor("hbc", [8, 16, 128], F32, kind="ExternalInput").ap()
    outs = {}
    for nm in ["r_o", "kk_o", "v_o", "b_o", "g_o", "k_o"]:
        outs[nm] = nc.dram_tensor(nm, [B, D, LC], BF16, kind="ExternalOutput").ap()
    w_o = nc.dram_tensor("w_o", [B, D, LC], F32, kind="ExternalOutput").ap()
    bonus_o = nc.dram_tensor("bonus_o", [B, D, LC], F32, kind="ExternalOutput").ap()
    rwb_h = nc.dram_tensor("rwb_h", [D, D], BF16, kind="Internal").ap()
    kwb_h = nc.dram_tensor("kwb_h", [D, D], BF16, kind="Internal").ap()
    vwb_h = nc.dram_tensor("vwb_h", [D, D], BF16, kind="Internal").ap()

    W = 512
    with ExitStack() as stack:
        E = stack.enter_context
        xt = E(nc.sbuf_tensor("xt", [128, 8 * W], F32))
        wbig = E(nc.sbuf_tensor("wbig", [128, 8 * D], BF16))
        xab = E(nc.sbuf_tensor("xab", [128, 8 * 192], BF16))
        xb2s = E(nc.sbuf_tensor("xb2s", [32, 6 * D], BF16))
        kab = E(nc.sbuf_tensor("kab", [128, 8 * 64], BF16))
        wab = E(nc.sbuf_tensor("wab", [128, 8 * 64], BF16))
        aab = E(nc.sbuf_tensor("aab", [128, 8 * 64], BF16))
        gab = E(nc.sbuf_tensor("gab", [128, 8 * 128], BF16))
        kbb = E(nc.sbuf_tensor("kbb", [64, D], BF16))
        wbb = E(nc.sbuf_tensor("wbb", [64, D], BF16))
        abb = E(nc.sbuf_tensor("abb", [64, D], BF16))
        gbb = E(nc.sbuf_tensor("gbb", [128, D], BF16))
        hselb = E(nc.sbuf_tensor("hselb", [128, 8 * 16], BF16))
        hbcb = E(nc.sbuf_tensor("hbcb", [16, 8 * 128], BF16))
        lamc = E(nc.sbuf_tensor("lamc", [128, 6 * 8], F32))
        xmc = E(nc.sbuf_tensor("xmc", [128, 8], F32))
        xtc = E(nc.sbuf_tensor("xtc", [128, 8], F32))
        wmc = E(nc.sbuf_tensor("wmc", [128, 8], F32))
        wmn = E(nc.sbuf_tensor("wmn", [128, 8], F32))
        atc = E(nc.sbuf_tensor("atc", [128, 8], F32))
        xh32 = E(nc.sbuf_tensor("xh32", [128, 8 * W], F32))
        xhb = E(nc.sbuf_tensor("xhb", [128, 8 * W], BF16))
        dx = E(nc.sbuf_tensor("dx", [128, 8 * W], F32))
        prevc = E(nc.sbuf_tensor("prevc", [128, 8], F32))
        xm = E(nc.sbuf_tensor("xm", [128, 8 * W], BF16))
        mixg = E(nc.sbuf_tensor("mixg", [32, 6 * W], BF16))
        xdg = E(nc.sbuf_tensor("xdg", [128, 8 * W], BF16))
        kfull = E(nc.sbuf_tensor("kfull", [128, 8 * W], BF16))
        vfull = E(nc.sbuf_tensor("vfull", [128, 8 * W], BF16))
        rfull = E(nc.sbuf_tensor("rfull", [128, 8 * W], BF16))
        wfull = E(nc.sbuf_tensor("wfull", [128, 8 * W], F32))
        t1 = E(nc.sbuf_tensor("t1", [128, W], F32))
        t2 = E(nc.sbuf_tensor("t2", [128, W], F32))
        t3 = E(nc.sbuf_tensor("t3", [128, W], BF16))
        t64 = E(nc.sbuf_tensor("t64", [64, W], BF16))
        t128 = E(nc.sbuf_tensor("t128", [128, W], BF16))
        rrow = E(nc.sbuf_tensor("rrow", [1, 2 * W], F32))
        x1t = E(nc.sbuf_tensor("x1t", [128, 24], F32))
        x1h = E(nc.sbuf_tensor("x1h", [128, 24], BF16))
        r1r = E(nc.sbuf_tensor("r1r", [1, 24], F32))
        ps_all = E(nc.psum_tensor("ps_all", [128, 8 * 512], F32))
        ln = LNKit(nc, E, W, bank=ps_all[:, 2048:2048 + 2 * W], sqb=xm, srcb=xdg, tmpf=dx)
        ps_a = ps_all[:, 0:512]
        ps_b = ps_all[:, 512:1024]
        ps_c = ps_all[:, 1024:1536]
        ps_d = ps_all[:, 1536:2048]

        s = Seq()
        ln.emit_init(s)
        for src, dst in ((rWT, rwb_h), (kWT, kwb_h), (vWT, vwb_h)):
            for c in range(8):
                cc = c
                s.em("sync", lambda e, cc=cc, src=src: e.dma_start(
                    out=xt[:, 0:D], in_=src[cc * 128:(cc + 1) * 128, :]), dma=True)
                s.brk()
                s.em("vector", lambda v: v.tensor_copy(t128[:, :], xt[:, 0:W]))
                s.em("vector", lambda v: v.tensor_copy(t3[:, :], xt[:, W:2 * W]))
                s.em("gpsimd", lambda g, cc=cc, dst=dst: g.dma_start(
                    out=dst[cc * 128:(cc + 1) * 128, 0:W], in_=t128[:, :]), dma=True)
                s.em("gpsimd", lambda g, cc=cc, dst=dst: g.dma_start(
                    out=dst[cc * 128:(cc + 1) * 128, W:2 * W], in_=t3[:, :]), dma=True)
        def cast_in(dst, src, width):
            rows = dst.shape[0]
            s.em("sync", lambda e: e.dma_start(out=xt[0:rows, 0:width], in_=src), dma=True)
            s.brk()
            s.em("vector", lambda v: v.tensor_copy(dst, xt[0:rows, 0:width]))
            s.brk()
        for c in range(8):
            cc = c
            cast_in(xab[:, cc * 192:(cc + 1) * 192], x_A[cc * 128:(cc + 1) * 128, :], 192)
            cast_in(kab[:, cc * 64:(cc + 1) * 64], k_A[cc * 128:(cc + 1) * 128, :], 64)
            cast_in(wab[:, cc * 64:(cc + 1) * 64], w_A[cc * 128:(cc + 1) * 128, :], 64)
            cast_in(aab[:, cc * 64:(cc + 1) * 64], a_A[cc * 128:(cc + 1) * 128, :], 64)
            cast_in(gab[:, cc * 128:(cc + 1) * 128], g_A[cc * 128:(cc + 1) * 128, :], 128)
            cast_in(hselb[:, cc * 16:(cc + 1) * 16], hsel[cc], 16)
            cast_in(hbcb[:, cc * 128:(cc + 1) * 128], hbc[cc], 128)
        for g_ in range(6):
            gg = g_
            cast_in(xb2s[:, gg * D:(gg + 1) * D], x_B2[gg], D)
        cast_in(kbb[:, :], k_B[:, :], D)
        cast_in(wbb[:, :], w_B[:, :], D)
        cast_in(abb[:, :], a_B[:, :], D)
        cast_in(gbb[:, :], g_B[:, :], D)
        s.em("sync", lambda e: e.dma_start(out=lamc[:, :].rearrange("p (g c) -> p g c", g=6),
             in_=lam.rearrange("g (c p) -> p g c", p=128), allow_slow_non_contiguous=True), dma=True)
        for nm, buf in (("xmiu", xmc), ("xtime", xtc), ("wmiu", wmc), ("atime", atc)):
            src = {"xmiu": xmiu, "xtime": xtime, "wmiu": wmiu, "atime": atime}[nm]
            s.em("sync", lambda e, src=src, buf=buf: e.dma_start(
                out=buf[:, :], in_=src.rearrange("(c p) -> p c", p=128),
                allow_slow_non_contiguous=True), dma=True)
        s.em("vector", lambda v: v.tensor_scalar_mul(wmn[:, :], wmc[:, :], -1.0))

        for b in range(B):
            s.em("sync", lambda e, b=b: e.dma_start(
                out=x1t[:, :].rearrange("p (c l) -> p c l", c=8),
                in_=xb1.rearrange("b (c p) l -> b p c l", p=128)[b],
                allow_slow_non_contiguous=True), dma=True)
            s.em("vector", lambda v: v.tensor_copy(x1h[:, :], x1t[:, :]))
            for c in range(8):
                cc = c
                s.em("tensor", lambda t, cc=cc: t.matmul(
                    ps_all[0:1, 2048:2051], ln.ones1[:, :], x1h[:, cc * 3:(cc + 1) * 3],
                    start=(cc == 0), stop=(cc == 7)))
            s.em("scalar", lambda sc: sc.activation(x1h[:, :], x1t[:, :], AF.Square))
            s.brk()
            for c in range(8):
                cc = c
                s.em("tensor", lambda t, cc=cc: t.matmul(
                    ps_all[0:1, 2560:2563], ln.ones1[:, :], x1h[:, cc * 3:(cc + 1) * 3],
                    start=(cc == 0), stop=(cc == 7)))
            mu1 = r1r[:, 0:3]; ms1 = r1r[:, 3:6]; va1 = r1r[:, 6:9]; rs1 = r1r[:, 9:12]; rb1 = r1r[:, 12:15]; tpx = r1r[:, 15:18]
            s.em("vector", lambda v: v.tensor_scalar_mul(mu1, ps_all[0:1, 2048:2051], 1.0 / D))
            s.em("vector", lambda v: v.tensor_scalar_mul(ms1, ps_all[0:1, 2560:2563], 1.0 / D))
            s.brk()
            s.em("vector", lambda v: v.tensor_mul(tpx, mu1, mu1))
            s.brk()
            s.em("vector", lambda v: v.tensor_sub(va1, ms1, tpx))
            s.brk()
            s.em("vector", lambda v: v.tensor_scalar_add(va1, va1, EPS))
            s.brk()
            s.em("scalar", lambda sc: sc.activation(va1, va1, AF.Sqrt))
            s.brk()
            s.em("vector", lambda v: v.reciprocal(rs1, va1))
            s.brk()
            s.em("vector", lambda v: v.tensor_mul(tpx, mu1, rs1))
            s.brk()
            s.em("vector", lambda v: v.tensor_scalar_mul(rb1, tpx, -1.0))
            s.brk()
            s.em("tensor", lambda t: t.matmul(ps_a[:, 0:3], ln.onesb[:, :], rs1, start=True, stop=True))
            s.em("tensor", lambda t: t.matmul(ps_b[:, 0:3], ln.onesb[:, :], rb1, start=True, stop=True))
            for c in range(8):
                cc = c
                sl = slice(cc * 3, (cc + 1) * 3)
                s.em("vector", lambda v, sl=sl: v.tensor_mul(x1t[:, sl], x1t[:, sl], ps_a[:, 0:3]))
                s.brk()
                s.em("vector", lambda v, sl=sl: v.tensor_add(x1h[:, sl], x1t[:, sl], ps_b[:, 0:3]))
                s.brk()
            s.em("vector", lambda v: v.tensor_copy(
                prevc[:, :].rearrange("p (c o) -> p c o", o=1),
                x1h[:, :].rearrange("p (c l) -> p c l", c=8)[:, :, 2:3]))
            s.brk()
            for h in range(2):
                c0 = h * W
                s.em("sync", lambda e, b=b, c0=c0: e.dma_start(
                    out=xt[:, :].rearrange("p (c l) -> p c l", c=8), in_=fmj(xsh, b, c0, W)), dma=True)
                ln.emit(s, xt, xhb, dst_f32=xh32)
                for c in range(8):
                    cc = c
                    s.em("vector", lambda v, cc=cc: v.tensor_copy(
                        dx[:, cc * W + 1: cc * W + W], xh32[:, cc * W: cc * W + W - 1]))
                s.em("vector", lambda v: v.tensor_copy(
                    dx[:, :].rearrange("p (c l) -> p c l", c=8)[:, :, 0:1],
                    prevc[:, :].rearrange("p (c o) -> p c o", o=1)))
                s.em("vector", lambda v: v.tensor_copy(
                    prevc[:, :].rearrange("p (c o) -> p c o", o=1),
                    xh32[:, :].rearrange("p (c l) -> p c l", c=8)[:, :, W - 1:W]))
                s.brk()
                s.em("vector", lambda v: v.tensor_sub(dx[:, :], dx[:, :], xh32[:, :]))
                for c in range(8):
                    cc = c
                    s.em("vector", lambda v, cc=cc: v.scalar_tensor_tensor(
                        t1[:, :], dx[:, cc * W:(cc + 1) * W], xmc[:, cc:cc + 1],
                        xh32[:, cc * W:(cc + 1) * W], ALU.mult, ALU.add))
                    s.em("vector", lambda v, cc=cc: v.tensor_copy(xm[:, cc * W:(cc + 1) * W], t1[:, :]))
                for g_ in range(6):
                    gg = g_
                    pb = ps_a if g_ % 2 == 0 else ps_b
                    for c in range(8):
                        cc = c
                        s.em("tensor", lambda t, gg=gg, cc=cc, pb=pb: t.matmul(
                            pb[0:32, :], xab[:, cc * 192 + gg * 32: cc * 192 + gg * 32 + 32],
                            xm[:, cc * W:(cc + 1) * W], start=(cc == 0), stop=(cc == 7)))
                    s.em("scalar", lambda sc, gg=gg, pb=pb: sc.activation(
                        mixg[:, gg * W:(gg + 1) * W], pb[0:32, :], AF.Tanh))
                    s.brk()
                for g_ in range(6):
                    gg = g_
                    for c in range(8):
                        cc = c
                        s.em("tensor", lambda t, gg=gg, cc=cc: t.matmul(
                            ps_c[:, :], xb2s[:, gg * D + cc * 128: gg * D + cc * 128 + 128],
                            mixg[:, gg * W:(gg + 1) * W], start=True, stop=True))
                        s.em("vector", lambda v, gg=gg, cc=cc: v.scalar_tensor_tensor(
                            t1[:, :], ps_c[:, :], lamc[:, gg * 8 + cc: gg * 8 + cc + 1],
                            dx[:, cc * W:(cc + 1) * W], ALU.add, ALU.mult))
                        s.em("vector", lambda v, cc=cc: v.tensor_add(t2[:, :], t1[:, :], xh32[:, cc * W:(cc + 1) * W]))
                        s.em("vector", lambda v, cc=cc: v.tensor_copy(xdg[:, cc * W:(cc + 1) * W], t2[:, :]))
                        s.brk()
                    if gg == 0 or gg == 3:
                        src_h = rwb_h if gg == 0 else vwb_h
                        full = rfull if gg == 0 else vfull
                        oo = outs["r_o"] if gg == 0 else outs["v_o"]
                        for c in range(8):
                            cc = c
                            s.em("sync", lambda e, cc=cc, src_h=src_h: e.dma_start(
                                out=wbig[:, cc * D:(cc + 1) * D], in_=src_h[cc * 128:(cc + 1) * 128, :]), dma=True)
                        for dc in range(8):
                            dcc = dc
                            pb = ps_a if dc % 2 == 0 else ps_b
                            for c in range(8):
                                cc = c
                                s.em("tensor", lambda t, dcc=dcc, cc=cc, pb=pb: t.matmul(
                                    pb[:, :], wbig[:, cc * D + dcc * 128: cc * D + dcc * 128 + 128],
                                    xdg[:, cc * W:(cc + 1) * W], start=(cc == 0), stop=(cc == 7)))
                            s.em("vector", lambda v, dcc=dcc, pb=pb, full=full: v.tensor_copy(
                                full[:, dcc * W:(dcc + 1) * W], pb[:, :]))
                            s.em("gpsimd", lambda g, b=b, dcc=dcc, c0=c0, full=full, oo=oo: g.dma_start(
                                out=oo[b, dcc * 128:(dcc + 1) * 128, c0:c0 + W],
                                in_=full[:, dcc * W:(dcc + 1) * W]), dma=True)
                            s.brk()
                    elif gg == 1:
                        for c in range(8):
                            cc = c
                            s.em("tensor", lambda t, cc=cc: t.matmul(
                                ps_c[0:64, :], wab[:, cc * 64:(cc + 1) * 64],
                                xdg[:, cc * W:(cc + 1) * W], start=(cc == 0), stop=(cc == 7)))
                        s.em("scalar", lambda sc: sc.activation(t64[:, :], ps_c[0:64, :], AF.Tanh))
                        s.brk()
                        for dc in range(8):
                            dcc = dc
                            s.em("tensor", lambda t, dcc=dcc: t.matmul(
                                ps_d[:, :], wbb[:, dcc * 128:(dcc + 1) * 128], t64[:, :],
                                start=True, stop=True))
                            s.em("scalar", lambda sc, dcc=dcc: sc.activation(
                                t1[:, :], ps_d[:, :], AF.Abs, bias=wmc[:, dcc:dcc + 1]))
                            s.em("scalar", lambda sc: sc.activation(t1[:, :], t1[:, :], AF.Exp, scale=-1.0))
                            s.em("scalar", lambda sc: sc.activation(t1[:, :], t1[:, :], AF.Ln, bias=1.0))
                            s.em("scalar", lambda sc, dcc=dcc: sc.activation(
                                t2[:, :], ps_d[:, :], AF.Relu, scale=-1.0, bias=wmn[:, dcc:dcc + 1]))
                            s.em("vector", lambda v: v.tensor_add(t1[:, :], t1[:, :], t2[:, :]))
                            s.em("vector", lambda v, dcc=dcc: v.tensor_scalar(
                                wfull[:, dcc * W:(dcc + 1) * W], t1[:, :], -1.0, -0.5, ALU.mult, ALU.add))
                            s.em("gpsimd", lambda g, b=b, dcc=dcc, c0=c0: g.dma_start(
                                out=w_o[b, dcc * 128:(dcc + 1) * 128, c0:c0 + W],
                                in_=wfull[:, dcc * W:(dcc + 1) * W]), dma=True)
                            s.brk()
                    elif gg == 2:
                        for c in range(8):
                            cc = c
                            s.em("sync", lambda e, cc=cc: e.dma_start(
                                out=wbig[:, cc * D:(cc + 1) * D], in_=kwb_h[cc * 128:(cc + 1) * 128, :]), dma=True)
                        for c in range(8):
                            cc = c
                            s.em("tensor", lambda t, cc=cc: t.matmul(
                                ps_c[0:64, :], kab[:, cc * 64:(cc + 1) * 64],
                                xdg[:, cc * W:(cc + 1) * W], start=(cc == 0), stop=(cc == 7)))
                        s.em("scalar", lambda sc: sc.activation(t64[:, :], ps_c[0:64, :], AF.Tanh))
                        s.brk()
                        for dc in range(8):
                            dcc = dc
                            pb = ps_a if dc % 2 == 0 else ps_b
                            for c in range(8):
                                cc = c
                                s.em("tensor", lambda t, dcc=dcc, cc=cc, pb=pb: t.matmul(
                                    pb[:, :], wbig[:, cc * D + dcc * 128: cc * D + dcc * 128 + 128],
                                    xdg[:, cc * W:(cc + 1) * W], start=(cc == 0), stop=False))
                            s.em("tensor", lambda t, dcc=dcc, pb=pb: t.matmul(
                                pb[:, :], kbb[:, dcc * 128:(dcc + 1) * 128], t64[:, :],
                                start=False, stop=True))
                            s.em("scalar", lambda sc, dcc=dcc, pb=pb: sc.activation(
                                xm[:, dcc * W:(dcc + 1) * W], pb[:, :], AF.Square))
                            s.em("vector", lambda v, dcc=dcc, pb=pb: v.tensor_copy(
                                kfull[:, dcc * W:(dcc + 1) * W], pb[:, :]))
                            s.brk()
                        for c in range(8):
                            cc = c
                            s.em("tensor", lambda t, cc=cc: t.matmul(
                                ps_all[0:1, 2048:2560], ln.ones1[:, :], xm[:, cc * W:(cc + 1) * W],
                                start=(cc == 0), stop=(cc == 7)))
                        nr = rrow[:, 0:W]
                        rn = rrow[:, W:2 * W]
                        s.em("scalar", lambda sc: sc.activation(nr, ps_all[0:1, 2048:2560], AF.Sqrt))
                        s.brk()
                        s.em("vector", lambda v: v.tensor_scalar_max(nr, nr, 1e-12))
                        s.em("vector", lambda v: v.reciprocal(rn, nr))
                        s.brk()
                        s.em("tensor", lambda t: t.matmul(ps_d[:, :], ln.onesb[:, :], rn, start=True, stop=True))
                        for dc in range(8):
                            dcc = dc
                            s.em("vector", lambda v, dcc=dcc: v.tensor_mul(
                                t1[:, :], kfull[:, dcc * W:(dcc + 1) * W], ps_d[:, :]))
                            s.em("vector", lambda v, dcc=dcc: v.tensor_copy(kfull[:, dcc * W:(dcc + 1) * W], t1[:, :]))
                            s.em("gpsimd", lambda g, b=b, dcc=dcc, c0=c0: g.dma_start(
                                out=outs["k_o"][b, dcc * 128:(dcc + 1) * 128, c0:c0 + W],
                                in_=kfull[:, dcc * W:(dcc + 1) * W]), dma=True)
                            s.brk()
                    elif gg == 4:
                        for c in range(8):
                            cc = c
                            s.em("tensor", lambda t, cc=cc: t.matmul(
                                ps_c[0:64, :], aab[:, cc * 64:(cc + 1) * 64],
                                xdg[:, cc * W:(cc + 1) * W], start=(cc == 0), stop=(cc == 7)))
                        s.em("vector", lambda v: v.tensor_copy(t64[:, :], ps_c[0:64, :]))
                        s.brk()
                        for dc in range(8):
                            dcc = dc
                            s.em("tensor", lambda t, dcc=dcc: t.matmul(
                                ps_d[:, :], abb[:, dcc * 128:(dcc + 1) * 128], t64[:, :],
                                start=True, stop=True))
                            s.em("scalar", lambda sc, dcc=dcc: sc.activation(
                                t1[:, :], ps_d[:, :], AF.Sigmoid, bias=atc[:, dcc:dcc + 1]))
                            s.em("vector", lambda v: v.tensor_scalar_mul(t1[:, :], t1[:, :], 2.0))
                            s.em("vector", lambda v, dcc=dcc: v.tensor_mul(
                                t3[:, :], kfull[:, dcc * W:(dcc + 1) * W], t1[:, :]))
                            s.em("gpsimd", lambda g, b=b, dcc=dcc, c0=c0: g.dma_start(
                                out=outs["b_o"][b, dcc * 128:(dcc + 1) * 128, c0:c0 + W],
                                in_=t3[:, :]), dma=True)
                            s.brk()
                    elif gg == 5:
                        for c in range(8):
                            cc = c
                            s.em("tensor", lambda t, cc=cc: t.matmul(
                                ps_c[:, :], gab[:, cc * 128:(cc + 1) * 128],
                                xdg[:, cc * W:(cc + 1) * W], start=(cc == 0), stop=(cc == 7)))
                        s.em("scalar", lambda sc: sc.activation(t128[:, :], ps_c[:, :], AF.Tanh))
                        s.brk()
                        for dc in range(8):
                            dcc = dc
                            s.em("tensor", lambda t, dcc=dcc: t.matmul(
                                ps_d[:, :], gbb[:, dcc * 128:(dcc + 1) * 128], t128[:, :],
                                start=True, stop=True))
                            s.em("vector", lambda v, dcc=dcc, pb=ps_d: v.tensor_copy(t3[:, :], pb[:, :]))
                            s.em("gpsimd", lambda g, b=b, dcc=dcc, c0=c0: g.dma_start(
                                out=outs["g_o"][b, dcc * 128:(dcc + 1) * 128, c0:c0 + W],
                                in_=t3[:, :]), dma=True)
                            s.brk()
                for dc in range(8):
                    dcc = dc
                    s.em("vector", lambda v, dcc=dcc: v.tensor_scalar(
                        t1[:, :], wfull[:, dcc * W:(dcc + 1) * W], 0.5, 0.0, ALU.mult, ALU.min))
                    s.em("scalar", lambda sc: sc.activation(t1[:, :], t1[:, :], AF.Exp))
                    s.brk()
                    s.em("vector", lambda v, dcc=dcc: v.tensor_mul(
                        t3[:, :], kfull[:, dcc * W:(dcc + 1) * W], t1[:, :]))
                    s.em("gpsimd", lambda g, b=b, dcc=dcc, c0=c0: g.dma_start(
                        out=outs["kk_o"][b, dcc * 128:(dcc + 1) * 128, c0:c0 + W], in_=t3[:, :]), dma=True)
                    s.brk()
                for dc in range(8):
                    dcc = dc
                    s.em("vector", lambda v, dcc=dcc: v.tensor_mul(
                        t1[:, :], rfull[:, dcc * W:(dcc + 1) * W], kfull[:, dcc * W:(dcc + 1) * W]))
                    s.em("vector", lambda v, dcc=dcc: v.tensor_scalar_mul(t1[:, :], t1[:, :], xtc[:, dcc:dcc + 1]))
                    s.em("vector", lambda v, dcc=dcc: v.tensor_copy(xm[:, dcc * W:(dcc + 1) * W], t1[:, :]))
                    s.brk()
                for c in range(8):
                    cc = c
                    s.em("tensor", lambda t, cc=cc: t.matmul(
                        ps_c[0:16, :], hselb[:, cc * 16:(cc + 1) * 16], xm[:, cc * W:(cc + 1) * W],
                        start=(cc == 0), stop=(cc == 7)))
                s.em("vector", lambda v: v.tensor_copy(t64[0:16, :].bitcast(F32) if False else t64[0:16, :], ps_c[0:16, :]))
                s.brk()
                for dc in range(8):
                    dcc = dc
                    s.em("tensor", lambda t, dcc=dcc: t.matmul(
                        ps_d[:, :], hbcb[:, dcc * 128:(dcc + 1) * 128], t64[0:16, :],
                        start=True, stop=True))
                    s.em("vector", lambda v, dcc=dcc: v.tensor_mul(
                        t1[:, :], vfull[:, dcc * W:(dcc + 1) * W], ps_d[:, :]))
                    s.em("gpsimd", lambda g, b=b, dcc=dcc, c0=c0: g.dma_start(
                        out=bonus_o[b, dcc * 128:(dcc + 1) * 128, c0:c0 + W], in_=t1[:, :]), dma=True)
                    s.brk()

        blk = E(nc.Block())
        s.build(nc, blk, stack)
    return nc



def build_wkv2():
    nc = bass.Bass("TRN2", target_bir_lowering=False)
    LT = NC * LC
    rP = nc.dram_tensor("rP", [8, 64, LT], BF16, kind="ExternalInput").ap()
    kkP = nc.dram_tensor("kkP", [8, 64, LT], BF16, kind="ExternalInput").ap()
    bP = nc.dram_tensor("bP", [8, 64, LT], BF16, kind="ExternalInput").ap()
    vP = nc.dram_tensor("vP", [8, 64, LT], BF16, kind="ExternalInput").ap()
    kP = nc.dram_tensor("kP", [8, 64, LT], BF16, kind="ExternalInput").ap()
    wP = nc.dram_tensor("wP", [8, 64, LT], F32, kind="ExternalInput").ap()
    identb2 = nc.dram_tensor("identb2", [128, 64], F32, kind="ExternalInput").ap()
    maskL2 = nc.dram_tensor("maskL2", [128, 64], F32, kind="ExternalInput").ap()
    maskU2 = nc.dram_tensor("maskU2", [128, 128], F32, kind="ExternalInput").ap()
    y_o = nc.dram_tensor("y_o", [8, LT, 64], F32, kind="ExternalOutput").ap()

    Q = 2048
    CH = Q // 64
    with ExitStack() as stack:
        E = stack.enter_context
        idb = E(nc.sbuf_tensor("idb", [128, 64], BF16))
        mskL = E(nc.sbuf_tensor("mskL", [128, 64], F32))
        mskU = E(nc.sbuf_tensor("mskU", [128, 128], F32))
        stg = E(nc.sbuf_tensor("stg", [128, 128], F32))
        kbuf = E(nc.sbuf_tensor("kbuf", [128, Q], BF16))
        kkb = E(nc.sbuf_tensor("kkb", [128, Q], BF16))
        bbf = E(nc.sbuf_tensor("bbf", [128, Q], BF16))
        vbf = E(nc.sbuf_tensor("vbf", [128, Q], BF16))
        rbf = E(nc.sbuf_tensor("rbf", [128, Q], BF16))
        wbf = E(nc.sbuf_tensor("wbf", [128, Q], F32))
        cw = E(nc.sbuf_tensor("cw", [128, Q], F32))
        zer = E(nc.sbuf_tensor("zer", [128, Q], F32))
        rel = E(nc.sbuf_tensor("rel", [128, Q], F32))
        rep = E(nc.sbuf_tensor("rep", [128, Q], F32))
        epos = E(nc.sbuf_tensor("epos", [128, Q], F32))
        eneg = E(nc.sbuf_tensor("eneg", [128, Q], F32))
        ew = E(nc.sbuf_tensor("ew", [128, Q], F32))
        eexc = E(nc.sbuf_tensor("eexc", [128, Q], F32))
        ubuf = E(nc.sbuf_tensor("ubuf", [128, Q], BF16))
        btil = E(nc.sbuf_tensor("btil", [128, Q], BF16))
        ktil = E(nc.sbuf_tensor("ktil", [128, Q], BF16))
        qtil = E(nc.sbuf_tensor("qtil", [128, Q], BF16))
        bhat = E(nc.sbuf_tensor("bhat", [128, Q], BF16))
        khat = E(nc.sbuf_tensor("khat", [128, Q], BF16))
        gcol = E(nc.sbuf_tensor("gcol", [128, 4 * CH], F32))
        us0 = E(nc.sbuf_tensor("us0", [128, Q], BF16))
        qs0 = E(nc.sbuf_tensor("qs0", [128, Q], BF16))
        Tm = E(nc.sbuf_tensor("Tm", [128, 2 * 64], BF16))
        TmT = E(nc.sbuf_tensor("TmT", [128, 2 * 64], BF16))
        mT_qb = E(nc.sbuf_tensor("mT_qb", [128, 128], BF16))
        mM_qk = E(nc.sbuf_tensor("mM_qk", [128, 128], BF16))
        Vt = E(nc.sbuf_tensor("Vt", [128, 64], BF16))
        Bt = E(nc.sbuf_tensor("Bt", [128, 64], BF16))
        Kt = E(nc.sbuf_tensor("Kt", [128, 64], BF16))
        G32 = E(nc.sbuf_tensor("G32", [128, 128], F32))
        Gb = E(nc.sbuf_tensor("Gb", [128, 128], BF16))
        S32 = E(nc.sbuf_tensor("S32", [128, 64], F32))
        Sb = E(nc.sbuf_tensor("Sb", [128, 64], BF16))
        Stmp = E(nc.sbuf_tensor("Stmp", [128, 64], F32))
        yev = E(nc.sbuf_tensor("yev", [128, 64], F32))
        ps1 = E(nc.psum_tensor("ps1", [128, 512], F32))
        ps2 = E(nc.psum_tensor("ps2", [128, 512], F32))

        s = Seq()
        s.em("sync", lambda e: e.dma_start(out=stg[:, 0:64], in_=identb2[:, :]), dma=True)
        s.em("sync", lambda e: e.dma_start(out=mskL[:, :], in_=maskL2[:, :]), dma=True)
        s.em("sync", lambda e: e.dma_start(out=mskU[:, :], in_=maskU2[:, :]), dma=True)
        s.em("vector", lambda v: v.tensor_copy(idb[:, :], stg[:, 0:64]))
        s.em("vector", lambda v: v.memset(zer[:, :], 0.0))

        pA_ps1 = ps1
        for pp in range(4):
            pa, pb_ = 2 * pp, 2 * pp + 1
            s.em("vector", lambda v: v.memset(S32[:, :], 0.0))
            s.em("vector", lambda v: v.memset(Sb[:, :], 0.0))
            for q in range(4):
                l0 = q * Q
                for (dst, src, half) in ((rbf, rP, 0), (kkb, kkP, 0), (bbf, bP, 0), (vbf, vP, 0), (kbuf, kP, 0)):
                    s.em("sync", lambda e, dst=dst, src=src, pa=pa, l0=l0: e.dma_start(
                        out=dst[0:64, :], in_=src[pa, :, l0:l0 + Q]), dma=True)
                    s.em("sync", lambda e, dst=dst, src=src, pb_=pb_, l0=l0: e.dma_start(
                        out=dst[64:128, :], in_=src[pb_, :, l0:l0 + Q]), dma=True)
                s.em("sync", lambda e, pa=pa, l0=l0: e.dma_start(out=wbf[0:64, :], in_=wP[pa, :, l0:l0 + Q]), dma=True)
                s.em("sync", lambda e, pb_=pb_, l0=l0: e.dma_start(out=wbf[64:128, :], in_=wP[pb_, :, l0:l0 + Q]), dma=True)
                for c in range(CH):
                    cc = c
                    s.em("vector", lambda v, cc=cc: v.tensor_tensor_scan(
                        cw[:, cc * 64:(cc + 1) * 64], wbf[:, cc * 64:(cc + 1) * 64],
                        zer[:, 0:64], 0.0, ALU.add, ALU.add))
                s.brk()
                s.em("vector", lambda v: v.tensor_copy(gcol[:, CH:2 * CH], cw[:, 31::64]))
                s.brk()
                s.em("vector", lambda v: v.tensor_copy(
                    rep[:, :].rearrange("p (c l) -> p c l", c=CH),
                    gcol[:, CH:2 * CH].rearrange("p c -> p c ()" if False else "p (c o) -> p c o", o=1).broadcast_to((128, CH, 64))))
                s.em("vector", lambda v: v.tensor_sub(rel[:, :], cw[:, :], rep[:, :]))
                s.em("scalar", lambda sc: sc.activation(epos[:, :], rel[:, :], AF.Exp))
                s.em("scalar", lambda sc: sc.activation(eneg[:, :], rel[:, :], AF.Exp, scale=-1.0))
                s.em("scalar", lambda sc: sc.activation(ew[:, :], wbf[:, :], AF.Exp, scale=-1.0))
                s.brk()
                s.em("vector", lambda v: v.tensor_mul(eexc[:, :], epos[:, :], ew[:, :]))
                s.em("vector", lambda v: v.scalar_tensor_tensor(
                    ubuf[:, :], kbuf[:, :], -1.0, eexc[:, :], ALU.mult, ALU.mult))
                s.em("vector", lambda v: v.tensor_mul(btil[:, :], bbf[:, :], eneg[:, :]))
                s.em("vector", lambda v: v.tensor_mul(ktil[:, :], kkb[:, :], eneg[:, :]))
                s.em("vector", lambda v: v.tensor_mul(qtil[:, :], rbf[:, :], epos[:, :]))
                s.em("scalar", lambda sc: sc.activation(gcol[:, 0:CH], gcol[:, CH:2 * CH], AF.Exp))
                s.brk()
                s.em("vector", lambda v: v.tensor_copy(
                    rep[:, :].rearrange("p (c l) -> p c l", c=CH),
                    gcol[:, 0:CH].rearrange("p (c o) -> p c o", o=1).broadcast_to((128, CH, 64))))
                s.em("vector", lambda v: v.tensor_mul(us0[:, :], ubuf[:, :], rep[:, :]))
                s.em("vector", lambda v: v.tensor_mul(qs0[:, :], qtil[:, :], rep[:, :]))
                s.em("vector", lambda v: v.tensor_copy(gcol[:, 2 * CH:3 * CH], epos[:, 63::64]))
                s.brk()
                s.em("vector", lambda v: v.tensor_copy(
                    rep[:, :].rearrange("p (c l) -> p c l", c=CH),
                    gcol[:, 2 * CH:3 * CH].rearrange("p (c o) -> p c o", o=1).broadcast_to((128, CH, 64))))
                s.em("vector", lambda v: v.tensor_mul(bhat[:, :], btil[:, :], rep[:, :]))
                s.em("vector", lambda v: v.tensor_mul(khat[:, :], ktil[:, :], rep[:, :]))
                s.em("vector", lambda v: v.tensor_mul(gcol[:, 3 * CH:4 * CH], gcol[:, 2 * CH:3 * CH], gcol[:, 0:CH]))
                s.brk()
                for c in range(CH):
                    cc = c
                    cg = q * CH + c
                    sl = slice(cc * 64, (cc + 1) * 64)
                    for bp in (0, 64):
                        bsl = slice(bp, bp + 64)
                        s.em("tensor", lambda t, sl=sl, bsl=bsl: t.matmul(
                            ps1[bsl, 0:64], ubuf[bsl, sl], btil[bsl, sl], start=True, stop=True))
                        s.em("tensor", lambda t, sl=sl, bsl=bsl: t.matmul(
                            ps1[bsl, 64:128], ubuf[bsl, sl], idb[bsl, :], start=True, stop=True))
                        s.em("tensor", lambda t, sl=sl, bsl=bsl: t.matmul(
                            ps1[bsl, 128:192], btil[bsl, sl], ubuf[bsl, sl], start=True, stop=True))
                        s.em("tensor", lambda t, sl=sl, bsl=bsl: t.matmul(
                            ps1[bsl, 192:256], btil[bsl, sl], qtil[bsl, sl], start=True, stop=True))
                        s.em("tensor", lambda t, sl=sl, bsl=bsl: t.matmul(
                            ps1[bsl, 256:320], ktil[bsl, sl], ubuf[bsl, sl], start=True, stop=True))
                        s.em("tensor", lambda t, sl=sl, bsl=bsl: t.matmul(
                            ps1[bsl, 320:384], ktil[bsl, sl], qtil[bsl, sl], start=True, stop=True))
                        s.em("tensor", lambda t, sl=sl, bsl=bsl: t.matmul(
                            ps1[bsl, 384:448], vbf[bsl, sl], idb[bsl, :], start=True, stop=True))
                        s.em("tensor", lambda t, sl=sl, bsl=bsl: t.matmul(
                            ps1[bsl, 448:512], bhat[bsl, sl], idb[bsl, :], start=True, stop=True))
                        s.em("tensor", lambda t, sl=sl, bsl=bsl: t.matmul(
                            ps2[bsl, 0:64], khat[bsl, sl], idb[bsl, :], start=True, stop=True))
                    s.em("vector", lambda v: v.tensor_mul(stg[:, 0:64], ps1[:, 0:64], mskL[:, :]))
                    s.em("vector", lambda v: v.tensor_copy(Tm[:, 0:64], stg[:, 0:64]))
                    s.em("vector", lambda v: v.tensor_copy(G32[:, 64:128], ps1[:, 64:128]))
                    s.em("vector", lambda v: v.tensor_copy(Gb[:, 64:128], ps1[:, 64:128]))
                    s.em("vector", lambda v: v.tensor_mul(stg[:, :], ps1[:, 128:256], mskU[:, :]))
                    s.em("vector", lambda v: v.tensor_copy(mT_qb[:, :], stg[:, :]))
                    s.em("vector", lambda v: v.tensor_mul(stg[:, :], ps1[:, 256:384], mskU[:, :]))
                    s.em("vector", lambda v: v.tensor_copy(mM_qk[:, :], stg[:, :]))
                    s.em("vector", lambda v: v.tensor_copy(Vt[:, :], ps1[:, 384:448]))
                    s.em("vector", lambda v: v.tensor_copy(Bt[:, :], ps1[:, 448:512]))
                    s.em("vector", lambda v: v.tensor_copy(Kt[:, :], ps2[:, 0:64]))
                    s.em("vector", lambda v: v.tensor_copy(TmT[:, 0:64], mT_qb[:, 0:64]))
                    for bp in (0, 64):
                        bsl = slice(bp, bp + 64)
                        s.em("tensor", lambda t, sl=sl, bsl=bsl: t.matmul(
                            ps2[bsl, 64:128], mM_qk[bsl, 0:64], Vt[bsl, :], start=True, stop=False))
                        s.em("tensor", lambda t, sl=sl, bsl=bsl: t.matmul(
                            ps2[bsl, 64:128], us0[bsl, sl], Sb[bsl, :], start=False, stop=True))
                    s.em("vector", lambda v: v.tensor_copy(G32[:, 0:64], ps2[:, 64:128]))
                    s.em("vector", lambda v: v.tensor_copy(Gb[:, 0:64], ps2[:, 64:128]))
                    for lev in range(6):
                        cur = (lev % 2) * 64
                        nxt = ((lev + 1) % 2) * 64
                        for bp in (0, 64):
                            bsl = slice(bp, bp + 64)
                            s.em("tensor", lambda t, bsl=bsl, cur=cur: t.matmul(
                                ps2[bsl, 128:256], TmT[bsl, cur:cur + 64], Gb[bsl, :], start=True, stop=True))
                        if lev < 5:
                            for bp in (0, 64):
                                bsl = slice(bp, bp + 64)
                                s.em("tensor", lambda t, bsl=bsl, cur=cur: t.matmul(
                                    ps2[bsl, 256:320], TmT[bsl, cur:cur + 64], Tm[bsl, cur:cur + 64], start=True, stop=True))
                                s.em("tensor", lambda t, bsl=bsl, cur=cur: t.matmul(
                                    ps2[bsl, 320:384], Tm[bsl, cur:cur + 64], TmT[bsl, cur:cur + 64], start=True, stop=True))
                        s.em("vector", lambda v: v.tensor_add(G32[:, :], G32[:, :], ps2[:, 128:256]))
                        s.em("vector", lambda v: v.tensor_copy(Gb[:, :], G32[:, :]))
                        if lev < 5:
                            s.em("vector", lambda v, nxt=nxt: v.tensor_copy(Tm[:, nxt:nxt + 64], ps2[:, 256:320]))
                            s.em("vector", lambda v, nxt=nxt: v.tensor_copy(TmT[:, nxt:nxt + 64], ps2[:, 320:384]))
                    for bp in (0, 64):
                        bsl = slice(bp, bp + 64)
                        s.em("tensor", lambda t, sl=sl, bsl=bsl: t.matmul(
                            ps2[bsl, 384:448], qs0[bsl, sl], Sb[bsl, :], start=True, stop=False))
                        s.em("tensor", lambda t, bsl=bsl: t.matmul(
                            ps2[bsl, 384:448], mT_qb[bsl, 64:128], Gb[bsl, 0:64], start=False, stop=False))
                        s.em("tensor", lambda t, bsl=bsl: t.matmul(
                            ps2[bsl, 384:448], mM_qk[bsl, 64:128], Vt[bsl, :], start=False, stop=True))
                        s.em("tensor", lambda t, bsl=bsl: t.matmul(
                            ps2[bsl, 448:512], Bt[bsl, :], Gb[bsl, 0:64], start=True, stop=False))
                        s.em("tensor", lambda t, bsl=bsl: t.matmul(
                            ps2[bsl, 448:512], Kt[bsl, :], Vt[bsl, :], start=False, stop=True))
                    s.em("vector", lambda v: v.tensor_copy(yev[:, :], ps2[:, 384:448]))
                    s.em("vector", lambda v, cg=cg: v.scalar_tensor_tensor(
                        Stmp[:, :], S32[:, :], gcol[:, 3 * CH + (cg % CH): 3 * CH + (cg % CH) + 1],
                        ps2[:, 448:512], ALU.mult, ALU.add))
                    s.em("vector", lambda v: v.tensor_copy(S32[:, :], Stmp[:, :]))
                    s.em("vector", lambda v: v.tensor_copy(Sb[:, :], S32[:, :]))
                    s.em("gpsimd", lambda g, pa=pa, cg=cg: g.dma_start(
                        out=y_o[pa, cg * 64:(cg + 1) * 64, :], in_=yev[0:64, :]), dma=True)
                    s.em("gpsimd", lambda g, pb_=pb_, cg=cg: g.dma_start(
                        out=y_o[pb_, cg * 64:(cg + 1) * 64, :], in_=yev[64:128, :]), dma=True)

        blk = E(nc.Block())
        s.build(nc, blk, stack)
    return nc



def build_wkv3():
    nc = bass.Bass("TRN2", target_bir_lowering=False)
    xsh = nc.dram_tensor("xsh", [B, D, LC], F32, kind="ExternalInput").ap()
    y_in = nc.dram_tensor("y_in", [B, D, LC], F32, kind="ExternalInput").ap()
    bonus_in = nc.dram_tensor("bonus_in", [B, D, LC], F32, kind="ExternalInput").ap()
    g_in = nc.dram_tensor("g_in", [B, D, LC], BF16, kind="ExternalInput").ap()
    gng = nc.dram_tensor("gng", [D], F32, kind="ExternalInput").ap()
    gnb = nc.dram_tensor("gnb", [D], F32, kind="ExternalInput").ap()
    oWT = nc.dram_tensor("oWT", [D, D], F32, kind="ExternalInput").ap()
    hsel = nc.dram_tensor("hsel", [8, 128, 16], F32, kind="ExternalInput").ap()
    hbc = nc.dram_tensor("hbc", [8, 16, 128], F32, kind="ExternalInput").ap()
    xout = nc.dram_tensor("xout", [B, D, LC], F32, kind="ExternalOutput").ap()

    W = 512
    GEPS = 1e-5 * 16
    with ExitStack() as stack:
        E = stack.enter_context
        wst = E(nc.sbuf_tensor("wst", [128, D], F32))
        owb = E(nc.sbuf_tensor("owb", [128, 8 * D], BF16))
        hselb = E(nc.sbuf_tensor("hselb", [128, 8 * 16], BF16))
        hbcf = E(nc.sbuf_tensor("hbcf", [16, 8 * 128], F32))
        gc_ = E(nc.sbuf_tensor("gc_", [128, 8], F32))
        bc_ = E(nc.sbuf_tensor("bc_", [128, 8], F32))
        xt = E(nc.sbuf_tensor("xt", [128, 8 * W], F32))
        yt = E(nc.sbuf_tensor("yt", [128, 8 * W], F32))
        ysq = E(nc.sbuf_tensor("ysq", [128, 8 * W], BF16))
        bt = E(nc.sbuf_tensor("bt", [128, 8 * W], F32))
        gt = E(nc.sbuf_tensor("gt", [128, 8 * W], BF16))
        ybg = E(nc.sbuf_tensor("ybg", [128, 8 * W], BF16))
        st16 = E(nc.sbuf_tensor("st16", [16, 6 * W], F32))
        t1 = E(nc.sbuf_tensor("t1", [128, W], F32))
        t2 = E(nc.sbuf_tensor("t2", [128, W], F32))
        ot = E(nc.sbuf_tensor("ot", [128, 8 * W], F32))
        ps_a = E(nc.psum_tensor("ps_a", [128, 512], F32))
        ps_b = E(nc.psum_tensor("ps_b", [128, 512], F32))
        ps_c = E(nc.psum_tensor("ps_c", [128, 512], F32))
        ps_d = E(nc.psum_tensor("ps_d", [128, 512], F32))
        ps_s = E(nc.psum_tensor("ps_s", [16, 512], F32))
        ps_q = E(nc.psum_tensor("ps_q", [16, 512], F32))

        s = Seq()

        def cast_in(dst, src, width):
            rows = dst.shape[0]
            s.em("sync", lambda e: e.dma_start(out=wst[0:rows, 0:width], in_=src), dma=True)
            s.brk()
            s.em("vector", lambda v: v.tensor_copy(dst, wst[0:rows, 0:width]))
            s.brk()
        for c in range(8):
            cc = c
            cast_in(owb[:, cc * D:(cc + 1) * D], oWT[cc * 128:(cc + 1) * 128, :], D)
            cast_in(hselb[:, cc * 16:(cc + 1) * 16], hsel[cc], 16)
            s.em("sync", lambda e, cc=cc: e.dma_start(
                out=hbcf[:, cc * 128:(cc + 1) * 128], in_=hbc[cc]), dma=True)
        s.em("sync", lambda e: e.dma_start(out=gc_[:, :], in_=gng.rearrange("(c p) -> p c", p=128),
             allow_slow_non_contiguous=True), dma=True)
        s.em("sync", lambda e: e.dma_start(out=bc_[:, :], in_=gnb.rearrange("(c p) -> p c", p=128),
             allow_slow_non_contiguous=True), dma=True)

        for ti in range(TOK // W):
            b = ti // 2
            c0 = (ti % 2) * W
            s.em("sync", lambda e, b=b, c0=c0: e.dma_start(
                out=xt[:, :].rearrange("p (c l) -> p c l", c=8), in_=fmj(xsh, b, c0, W)), dma=True)
            s.em("sync", lambda e, b=b, c0=c0: e.dma_start(
                out=yt[:, :].rearrange("p (c l) -> p c l", c=8), in_=fmj(y_in, b, c0, W)), dma=True)
            s.em("sync", lambda e, b=b, c0=c0: e.dma_start(
                out=bt[:, :].rearrange("p (c l) -> p c l", c=8), in_=fmj(bonus_in, b, c0, W)), dma=True)
            s.em("sync", lambda e, b=b, c0=c0: e.dma_start(
                out=gt[:, :].rearrange("p (c l) -> p c l", c=8), in_=fmj(g_in, b, c0, W)), dma=True)
            s.em("vector", lambda v: v.tensor_copy(ysq[:, :], yt[:, :]))
            for c in range(8):
                cc = c
                s.em("tensor", lambda t, cc=cc: t.matmul(
                    ps_s[:, :], hselb[:, cc * 16:(cc + 1) * 16], ysq[:, cc * W:(cc + 1) * W],
                    start=(cc == 0), stop=(cc == 7)))
            s.em("scalar", lambda sc: sc.activation(ysq[:, :], yt[:, :], AF.Square))
            s.brk()
            for c in range(8):
                cc = c
                s.em("tensor", lambda t, cc=cc: t.matmul(
                    ps_q[:, :], hselb[:, cc * 16:(cc + 1) * 16], ysq[:, cc * W:(cc + 1) * W],
                    start=(cc == 0), stop=(cc == 7)))
            mu = st16[:, 0:W]; ms = st16[:, W:2 * W]; va = st16[:, 2 * W:3 * W]
            rs = st16[:, 3 * W:4 * W]; rb = st16[:, 4 * W:5 * W]; tp = st16[:, 5 * W:6 * W]
            s.em("vector", lambda v: v.tensor_scalar_mul(mu, ps_s[:, :], 1.0 / 64))
            s.em("vector", lambda v: v.tensor_scalar_mul(ms, ps_q[:, :], 1.0 / 64))
            s.em("vector", lambda v: v.tensor_mul(tp, mu, mu))
            s.em("vector", lambda v: v.tensor_sub(va, ms, tp))
            s.em("vector", lambda v: v.tensor_scalar_add(va, va, GEPS))
            s.em("scalar", lambda sc: sc.activation(va, va, AF.Sqrt))
            s.brk()
            s.em("vector", lambda v: v.reciprocal(rs, va))
            s.brk()
            s.em("vector", lambda v: v.tensor_mul(tp, mu, rs))
            s.em("vector", lambda v: v.tensor_scalar_mul(rb, tp, -1.0))
            for dc in range(8):
                dcc = dc
                s.em("tensor", lambda t, dcc=dcc: t.matmul(
                    ps_a[:, :], hbcf[:, dcc * 128:(dcc + 1) * 128], rs, start=True, stop=True))
                s.em("tensor", lambda t, dcc=dcc: t.matmul(
                    ps_b[:, :], hbcf[:, dcc * 128:(dcc + 1) * 128], rb, start=True, stop=True))
                s.em("vector", lambda v, dcc=dcc: v.tensor_mul(t1[:, :], yt[:, dcc * W:(dcc + 1) * W], ps_a[:, :]))
                s.em("vector", lambda v: v.tensor_add(t1[:, :], t1[:, :], ps_b[:, :]))
                s.em("vector", lambda v, dcc=dcc: v.tensor_scalar(
                    t2[:, :], t1[:, :], gc_[:, dcc:dcc + 1], bc_[:, dcc:dcc + 1], ALU.mult, ALU.add))
                s.em("vector", lambda v, dcc=dcc: v.tensor_add(t2[:, :], t2[:, :], bt[:, dcc * W:(dcc + 1) * W]))
                s.em("vector", lambda v, dcc=dcc: v.tensor_mul(ybg[:, dcc * W:(dcc + 1) * W], t2[:, :], gt[:, dcc * W:(dcc + 1) * W]))
                s.brk()
            for dc in range(8):
                dcc = dc
                pb = ps_c if dc % 2 == 0 else ps_d
                for c in range(8):
                    cc = c
                    s.em("tensor", lambda t, dcc=dcc, cc=cc, pb=pb: t.matmul(
                        pb[:, :], owb[:, cc * D + dcc * 128: cc * D + dcc * 128 + 128],
                        ybg[:, cc * W:(cc + 1) * W], start=(cc == 0), stop=(cc == 7)))
                s.em("vector", lambda v, dcc=dcc, pb=pb: v.tensor_add(
                    ot[:, dcc * W:(dcc + 1) * W], pb[:, :], xt[:, dcc * W:(dcc + 1) * W]))
                s.brk()
            s.em("gpsimd", lambda g, b=b, c0=c0: g.dma_start(
                out=fmj(xout, b, c0, W), in_=ot[:, :].rearrange("p (c l) -> p c l", c=8)), dma=True)

        blk = E(nc.Block())
        s.build(nc, blk, stack)
    return nc


def run_l1attn(x_fm, inp):
    D_ = D
    lam2 = np.ascontiguousarray(inp['L1_attn_lambdas'].reshape(6, D_))
    xB2 = np.ascontiguousarray(inp['L1_attn_x_B'])
    hsel = np.zeros((8, 128, 16), np.float32)
    hbc = np.zeros((8, 16, 128), np.float32)
    for dc in range(8):
        for p in range(128):
            h = (dc * 128 + p) // 64
            hsel[dc, p, h] = 1.0
            hbc[dc, h, p] = 1.0
    p10 = get_prog("wkv1", build_wkv1)
    in10 = []
    for c in range(NC):
        xb1 = np.zeros((B, D_, 3), np.float32) if c == 0 else np.ascontiguousarray(x_fm[c - 1][:, :, LC - 3:LC])
        in10.append(dict(
            xsh=x_fm[c], xb1=xb1, lam=lam2,
            x_A=inp['L1_attn_x_A'], x_B2=xB2,
            xmiu=np.ascontiguousarray(inp['L1_attn_x_miu'].reshape(D_)),
            xtime=np.ascontiguousarray(inp['L1_attn_x_time'].reshape(D_)),
            k_A=inp['L1_attn_k_A'], k_B=inp['L1_attn_k_B'],
            g_A=inp['L1_attn_g_A'], g_B=inp['L1_attn_g_B'],
            w_A=inp['L1_attn_w_A'], w_B=inp['L1_attn_w_B'],
            wmiu=np.ascontiguousarray(inp['L1_attn_w_miu'].reshape(D_)),
            a_A=inp['L1_attn_a_A'], a_B=inp['L1_attn_a_B'],
            atime=np.ascontiguousarray(inp['L1_attn_a_time'].reshape(D_)),
            rWT=np.ascontiguousarray(inp['L1_attn_rW'].T),
            kWT=np.ascontiguousarray(inp['L1_attn_kW'].T),
            vWT=np.ascontiguousarray(inp['L1_attn_vW'].T),
            hsel=hsel, hbc=hbc))
    r10 = run_bass_kernel_spmd(p10, in10, core_ids=list(range(NC)))
    outs10 = {nm: [r10.results[c][nm] for c in range(NC)] for nm in
              ["r_o", "kk_o", "v_o", "b_o", "g_o", "k_o", "w_o", "bonus_o"]}
    def pairs_for(core):
        return [(g // 16, g % 16) for g in range(core * 8, (core + 1) * 8)]
    def asm(nm, dtype):
        full = [np.concatenate([outs10[nm][k][b] for k in range(NC)], axis=1) for b in range(B)]
        out = []
        for c in range(NC):
            arr = np.stack([full[b][h * 64:(h + 1) * 64, :] for (b, h) in pairs_for(c)], 0)
            out.append(np.ascontiguousarray(arr))
        return out
    rPs = asm("r_o", BF); kkPs = asm("kk_o", BF); bPs = asm("b_o", BF)
    vPs = asm("v_o", BF); kPs = asm("k_o", BF); wPs = asm("w_o", np.float32)
    I64 = np.eye(64, dtype=np.float32)
    identb2 = np.vstack([I64, I64])
    mL = np.tril(np.ones((64, 64), np.float32), -1)
    mU = np.triu(np.ones((64, 64), np.float32), 1)
    mUi = np.triu(np.ones((64, 64), np.float32), 0)
    maskL2 = np.vstack([mL, mL])
    maskU2 = np.vstack([np.hstack([mU, mUi]), np.hstack([mU, mUi])])
    p11 = get_prog("wkv2", build_wkv2)
    in11 = [dict(rP=rPs[c], kkP=kkPs[c], bP=bPs[c], vP=vPs[c], kP=kPs[c], wP=wPs[c],
                 identb2=identb2, maskL2=maskL2, maskU2=maskU2) for c in range(NC)]
    r11 = run_bass_kernel_spmd(p11, in11, core_ids=list(range(NC)))
    y_by_core = [r11.results[c]["y_o"] for c in range(NC)]
    yfull = np.zeros((B, D, NC * LC), np.float32)
    for c in range(NC):
        for i, (b, h) in enumerate(pairs_for(c)):
            yfull[b, h * 64:(h + 1) * 64, :] = y_by_core[c][i].T
    p12 = get_prog("wkv3", build_wkv3)
    in12 = []
    for c in range(NC):
        in12.append(dict(
            xsh=x_fm[c], y_in=np.ascontiguousarray(yfull[:, :, c * LC:(c + 1) * LC]),
            bonus_in=outs10["bonus_o"][c], g_in=outs10["g_o"][c],
            gng=inp['L1_attn_gn_gamma'], gnb=inp['L1_attn_gn_beta'],
            oWT=np.ascontiguousarray(inp['L1_attn_oW'].T), hsel=hsel, hbc=hbc))
    r12 = run_bass_kernel_spmd(p12, in12, core_ids=list(range(NC)))
    return [r12.results[c]["xout"] for c in range(NC)]
